# version 02
# baseline (speedup 1.0000x reference)
import numpy as np
import concourse.bass as bass
import concourse.mybir as mybir
from concourse.ap import AP
from concourse.bass_utils import run_bass_kernel_spmd

F32 = mybir.dt.float32
BF16 = mybir.dt.bfloat16
I32 = mybir.dt.int32
AF = mybir.ActivationFunctionType
ALU = mybir.AluOpType
AX = mybir.AxisListType.X

D = 2048
DC = 16
T = 2048
TH = 1024
TB = 512
EPS = 1e-5
ENGS = ('pe', 'act', 'dve', 'gp', 'sy')
PADK = [16, 32, 48, 64]
WQ = 8192
PADZ = 8192
WK = PADZ + 8192
SS = 8224

class Sem:
    def __init__(self, nc, name):
        self.h = nc.alloc_semaphore(name)
        self.v = 0
        self.name = name


class K:
    def __init__(self, nc):
        self.nc = nc
        self.streams = {e: [] for e in ENGS}
        self.sems = {}
        self.bufs = {}
        self.psum = {}
        self.wslab_marks = [0, 0]
        self.dslab_marks = [0, 0]

    def sem(self, name):
        if name not in self.sems:
            self.sems[name] = Sem(self.nc, name)
        return self.sems[name]

    def op(self, eng, fn, wait=(), inc=None, inc_n=1):
        if inc is not None:
            inc.v += inc_n
        wait = [(s, v) for (s, v) in wait if v > 0]

        def run(h, fn=fn, wait=tuple(wait), inc=inc, inc_n=inc_n):
            for s, v in wait:
                h.wait_ge(s.h, v)
            ins = fn(h)
            if inc is not None:
                ins.then_inc(inc.h, inc_n)
        self.streams[eng].append(run)

    def drain(self, eng):
        self.streams[eng].append(lambda h: h.drain())

    def waitop(self, eng, waits):
        for s, v in waits:
            if v > 0:
                self.streams[eng].append(
                    lambda h, s=s, v=v: h.wait_ge(s.h, v))

    def finalize(self, blk):
        st = self.streams
        blk.tensor(lambda h: [f(h) for f in st['pe']])
        blk.scalar(lambda h: [f(h) for f in st['act']])
        blk.vector(lambda h: [f(h) for f in st['dve']])
        blk.gpsimd(lambda h: [f(h) for f in st['gp']])
        blk.sync(lambda h: [f(h) for f in st['sy']])


    CONST_VALUES = (1e-5, -0.5, -1.0, 0.5, 17.0, -17.0, 1e-30, 1.0 / 16.0)

    def emit_const_init(self, sel8_hbm, id_hbm):
        nc = self.nc
        b = self.bufs
        V, S = self.sem('V'), self.sem('S')
        self.op('sy', lambda h: h.dma_start(out=b['sel8'][:], in_=sel8_hbm), inc=S, inc_n=16)
        self.op('sy', lambda h: h.dma_start(out=b['id_f32'][:], in_=id_hbm), inc=S, inc_n=16)
        self.op('dve', lambda h: h.memset(b['ones_col_bf'][:], 1.0), inc=V)
        self.op('dve', lambda h: h.memset(b['ones_row_f32'][:], 1.0), inc=V)
        cv = nc.alloc_sbuf_tensor('cvals', [1, max(16, len(self.CONST_VALUES))], F32)[:]
        for i, val in enumerate(self.CONST_VALUES):
            ap = cv[0:1, i:i + 1]
            self.op('dve', lambda h, a=ap, v=val: h.memset(a, v), inc=V)
            nc.const_aps.register(F32, val, ap)
        self.init_done = (self.sem('S').v, self.sem('V').v)


    def emit_router(self, yT, rt_hbm, th):
        b = self.bufs
        P, A, V, W = self.sem('P'), self.sem('A'), self.sem('V'), self.sem('W')
        b4, b5, b6, b7 = self.psum['b4'], self.psum['b5'], self.psum['b6'], self.psum['b7']
        b2, b3 = self.psum['b2'], self.psum['b3']
        wsl = b['wslab'][0]
        self.op('gp', lambda h: h.dma_start(out=wsl[:, 0:144], in_=rt_hbm),
                wait=[(P, max(self.wslab_marks[0], getattr(self, 'gp_gate', 0)))],
                inc=W, inc_n=16)
        wv = W.v
        for dc in range(16):
            for tb in range(2):
                self.op('pe', lambda h, d=dc, t_=tb:
                        h.matmul((b4 if t_ == 0 else b5)[0:8, :],
                                 wsl[:, d * 8:(d + 1) * 8],
                                 yT[:, d * TH + t_ * TB: d * TH + (t_ + 1) * TB],
                                 start=(d == 0), stop=(d == 15)),
                        wait=[(W, wv)] if (dc == 0 and tb == 0) else ())
            for tb in range(2):
                self.op('pe', lambda h, d=dc, t_=tb:
                        h.matmul((b6 if t_ == 0 else b7)[0:1, :],
                                 wsl[:, 128 + d:129 + d],
                                 yT[:, d * TH + t_ * TB: d * TH + (t_ + 1) * TB],
                                 start=(d == 0), stop=(d == 15)),
                        inc=(P if (dc == 15 and tb == 1) else None))
        pv = P.v
        self.wslab_marks[0] = pv
        self.op('dve', lambda h: h.tensor_copy(b['rl'][0:8, 0:TB], b4[0:8, :]),
                wait=[(P, pv)], inc=V)
        self.op('dve', lambda h: h.tensor_copy(b['rl'][0:8, TB:TH], b5[0:8, :]), inc=V)
        vv = V.v
        srow = b['rows'][:, 0:2 * TB]
        self.op('act', lambda h: h.activation(srow[:, 0:TB], b6[0:1, :], AF.Sigmoid),
                wait=[(P, pv)], inc=A)
        self.op('act', lambda h: h.activation(srow[:, TB:2 * TB], b7[0:1, :], AF.Sigmoid),
                inc=A)
        av = A.v
        self.op('pe', lambda h: h.matmul(b2[:, :], b['ones_row_f32'][0:1, :],
                                         srow[:, 0:TB], start=True, stop=True),
                wait=[(A, av)])
        self.op('pe', lambda h: h.matmul(b3[:, :], b['ones_row_f32'][0:1, :],
                                         srow[:, TB:2 * TB], start=True, stop=True), inc=P)
        pv2 = P.v
        self.op('dve', lambda h: h.tensor_copy(b['sig_bc'][:, 0:TB], b2[:, :]),
                wait=[(P, pv2)], inc=V)
        self.op('dve', lambda h: h.tensor_copy(b['sig_bc'][:, TB:TH], b3[:, :]), inc=V)
        self.drain('dve')
        lt, lte, ltp, ltm, ltw = b['lt'], b['lte'], b['ltp'], b['ltm'], b['ltw']
        c = b['cols']
        for st_ in range(8):
            self.op('pe', lambda h, s=st_: h.transpose(
                b6[:, 0:8], b['rl'][0:8, s * 128:(s + 1) * 128], b['id_f32'][0:8, 0:8]),
                wait=[(V, V.v)], inc=P)
            pv3 = P.v
            self.op('dve', lambda h: h.tensor_copy(lt[:], b6[:, 0:8]),
                    wait=[(P, pv3)], inc=V)
            self.drain('dve')
            self.op('dve', lambda h: h.reduce_max(c[:, 0:1], lt[:], axis=AX, negate=True),
                    inc=V)
            self.op('act', lambda h: h.activation(lte[:], lt[:], AF.Exp, bias=c[:, 0:1]),
                    wait=[(V, V.v)], inc=A)
            self.op('dve', lambda h: h.reduce_sum(c[:, 1:2], lte[:], axis=AX),
                    wait=[(A, A.v)], inc=V)
            self.drain('dve')
            self.op('dve', lambda h: h.reciprocal(c[:, 2:3], c[:, 1:2]), inc=V)
            self.drain('dve')
            self.op('dve', lambda h: h.tensor_scalar_mul(ltp[:], lte[:], c[:, 2:3]), inc=V)
            self.drain('dve')
            self.op('dve', lambda h: h.reduce_max(c[:, 3:4], ltp[:], axis=AX), inc=V)
            self.drain('dve')
            self.op('dve', lambda h: h.tensor_scalar(ltm[:], ltp[:], c[:, 3:4], None,
                                                     ALU.is_ge), inc=V)
            self.drain('dve')
            self.op('dve', lambda h: h.scalar_tensor_tensor(lte[:], ltm[:], -1e30, ltp[:],
                                                            ALU.mult, ALU.add), inc=V)
            self.drain('dve')
            self.op('dve', lambda h: h.reduce_max(c[:, 4:5], lte[:], axis=AX), inc=V)
            self.drain('dve')
            self.op('dve', lambda h: h.tensor_scalar(ltm[:], ltp[:], c[:, 4:5], None,
                                                     ALU.is_ge), inc=V)
            self.drain('dve')
            self.op('dve', lambda h: h.tensor_mul(ltw[:], ltp[:], ltm[:]), inc=V)
            self.drain('dve')
            self.op('pe', lambda h: h.transpose(b7[0:8, 0:128], ltw[:], b['id_f32'][:, :]),
                    wait=[(V, V.v)], inc=P)
            pv4 = P.v
            self.op('dve', lambda h, s=st_: h.tensor_copy(
                b['w_rows'][0:8, s * 128:(s + 1) * 128], b7[0:8, 0:128]),
                wait=[(P, pv4)], inc=V)
        self.waitop('pe', [(V, V.v)])
        self.rt_done = V.v


    def emit_we_bcast(self, e):
        b = self.bufs
        P, V = self.sem('P'), self.sem('V')
        b2, b3 = self.psum['b2'], self.psum['b3']
        self.op('pe', lambda h, e=e: h.matmul(b2[:, :], b['sel8'][:, e * 128:(e + 1) * 128],
                                         b['w_rows'][0:8, 0:TB], start=True, stop=True))
        self.op('pe', lambda h, e=e: h.matmul(b3[:, :], b['sel8'][:, e * 128:(e + 1) * 128],
                                         b['w_rows'][0:8, TB:2 * TB], start=True, stop=True),
                inc=P)
        pv = P.v
        self.op('dve', lambda h: h.tensor_copy(b['we_bc'][:, 0:TB], b2[:, :]),
                wait=[(P, pv)], inc=V)
        self.op('dve', lambda h: h.tensor_copy(b['we_bc'][:, TB:TH], b3[:, :]), inc=V)
        self.drain('dve')
        self.waitop('pe', [(V, V.v)])


    def emit_quarter(self, xh, yT, gu_slabs, down_slabs, n_pairs, fold_sb, h_guard):
        b = self.bufs
        P, A, V, W = self.sem('P'), self.sem('A'), self.sem('V'), self.sem('W')
        wsl, dsl = b['wslab'], b['dslab']
        gu_hbm = gu_slabs
        GB = [[self.psum['b0'], self.psum['b1']], [self.psum['b4'], self.psum['b5']]]
        UB = [[self.psum['b2'], self.psum['b3']], [self.psum['b6'], self.psum['b7']]]
        slab_consumed = []
        sil_marks = []
        hm_marks = []
        for ip in range(n_pairs):
            par = ip % 2
            for role in range(2):
                sl = 2 * ip + role
                bi = sl % 2
                gw = [(P, slab_consumed[sl - 2])] if sl >= 2 else \
                     [(P, max(self.wslab_marks[bi], getattr(self, 'gp_gate', 0)))]
                self.op('gp', lambda h, s=sl, bi=bi:
                        h.dma_start(out=wsl[bi][:], in_=gu_hbm[s]),
                        wait=gw, inc=W, inc_n=16)
                wv = W.v
                pw = [(W, wv)]
                if role == 0 and ip >= 2:
                    pw = pw + [(A, sil_marks[2 * (ip - 2) + 1]),
                               (V, hm_marks[2 * (ip - 2) + 1])]
                for dc in range(16):
                    for tb in range(2):
                        bank = (GB if role == 0 else UB)[par][tb]
                        last = (dc == 15 and tb == 1)
                        self.op('pe', lambda h, bi=bi, d=dc, t_=tb, bk=bank,
                                st=(dc == 0), sp=(dc == 15):
                                h.matmul(bk[:, :], wsl[bi][:, d * 128:(d + 1) * 128],
                                         yT[:, d * TH + t_ * TB: d * TH + (t_ + 1) * TB],
                                         start=st, stop=sp),
                                wait=pw if (dc == 0 and tb == 0) else (),
                                inc=(P if last else None))
                slab_consumed.append(P.v)
                self.wslab_marks[bi] = P.v
            wc_last = slab_consumed[-1]
            for tb in range(2):
                tsl = b['tmp_silu'][:, (2 * par + tb) * TB:(2 * par + tb + 1) * TB]
                awq = [(P, wc_last)]
                if ip >= 2:
                    awq.append((V, hm_marks[2 * (ip - 2) + tb]))
                self.op('act', lambda h, o=tsl, g=GB[par][tb]:
                        h.activation(o, g[:, :], AF.Silu),
                        wait=awq, inc=A)
                sil_marks.append(A.v)
            for tb in range(2):
                tsl = b['tmp_silu'][:, (2 * par + tb) * TB:(2 * par + tb + 1) * TB]
                hsl = b['h'][:, ip * TH + tb * TB: ip * TH + (tb + 1) * TB]
                dw = [(A, sil_marks[2 * ip + tb])]
                if ip == 0 and tb == 0 and h_guard > 0:
                    dw = dw + [(P, h_guard)]
                self.op('dve', lambda h, o=hsl, a=tsl, u=UB[par][tb]:
                        h.tensor_mul(o, a, u[:, :]), wait=dw,
                        inc=(V if fold_sb is None else None))
                if fold_sb is None:
                    hm_marks.append(V.v)
            if fold_sb is not None:
                self.drain('dve')
                for tb in range(2):
                    hsl = b['h'][:, ip * TH + tb * TB: ip * TH + (tb + 1) * TB]
                    self.op('dve', lambda h, o=hsl, f=fold_sb, t_=tb:
                            h.tensor_mul(o, o, f[:, t_ * TB:(t_ + 1) * TB]), inc=V)
                    hm_marks.append(V.v)
        DBK = [[self.psum['b0'], self.psum['b1']], [self.psum['b2'], self.psum['b3']]]
        dsl_consumed = []
        add_marks = []
        for dc in range(16):
            par = dc % 2
            gw = [(P, dsl_consumed[dc - 2])] if dc >= 2 else \
                 [(P, max(self.dslab_marks[par], slab_consumed[-1]))]
            self.op('gp', lambda h, d=dc, bi=par, n=n_pairs:
                    h.dma_start(out=dsl[bi][:, 0:n * 128], in_=down_slabs[d]),
                    wait=gw, inc=W, inc_n=16)
            wv = W.v
            pw = [(W, wv)]
            if dc == 0:
                pw = pw + [(V, hm_marks[-1]), (A, sil_marks[-1])]
            if dc >= 2:
                pw = pw + [(V, add_marks[dc - 2])]
            for ic in range(n_pairs):
                for tb in range(2):
                    last = (ic == n_pairs - 1 and tb == 1)
                    self.op('pe', lambda h, bi=par, i=ic, t_=tb, bk=DBK[par][tb],
                            st=(ic == 0), sp=(ic == n_pairs - 1):
                            h.matmul(bk[:, :], dsl[bi][:, i * 128:(i + 1) * 128],
                                     b['h'][:, i * TH + t_ * TB: i * TH + (t_ + 1) * TB],
                                     start=st, stop=sp),
                            wait=pw if (ic == 0 and tb == 0) else (),
                            inc=(P if last else None))
            dsl_consumed.append(P.v)
            self.dslab_marks[par] = P.v
            for tb in range(2):
                xsl = xh[:, dc * TH + tb * TB: dc * TH + (tb + 1) * TB]
                self.op('dve', lambda h, x=xsl, bk=DBK[par][tb]:
                        h.tensor_add(x, x, bk[:, :]),
                        wait=[(P, dsl_consumed[-1])] if tb == 0 else (), inc=V)
            add_marks.append(V.v)
        self.waitop('pe', [(V, add_marks[-1]), (V, add_marks[-2])])
        self.gp_gate = dsl_consumed[-1]
        return dsl_consumed[-1]



def host_gu_slabs(gu, chunk_order):
    out = []
    for ic in chunk_order:
        A = gu[ic * 128:(ic + 1) * 128, :]
        out.append(np.ascontiguousarray(
            A.reshape(128, 16, 128).transpose(2, 1, 0).reshape(128, 2048)))
    return np.ascontiguousarray(np.stack(out))


def host_partial_slab(tail):
    R = tail.shape[0]
    return np.ascontiguousarray(
        tail.reshape(R, 16, 128).transpose(2, 1, 0).reshape(128, 16 * R))


def host_down_slabs(wd, n_ic):
    I = wd.shape[1]
    assert I == n_ic * 128
    out = []
    for dc in range(16):
        B = wd[dc * 128:(dc + 1) * 128, :]
        out.append(np.ascontiguousarray(
            B.reshape(128, n_ic, 128).transpose(2, 1, 0).reshape(128, I)))
    return np.ascontiguousarray(np.stack(out))


def emit_attn_consts(k, tau_hbm):
    b = k.bufs
    P, V, S = k.sem('P'), k.sem('V'), k.sem('S')
    tt = b['tinyf']
    k.op('sy', lambda h: h.dma_start(out=tt[0:1, 0:1], in_=tau_hbm), inc=S, inc_n=16)
    sv = S.v
    k.op('dve', lambda h: h.reciprocal(tt[0:1, 1:2], tt[0:1, 0:1]),
         wait=[(S, sv)], inc=V)
    k.drain('dve')
    vv = V.v
    k.op('pe', lambda h: h.matmul(k.psum['b0'][:, 0:1], b['ones_row_f32'][0:1, :],
                                  tt[0:1, 1:2], start=True, stop=True),
         wait=[(V, vv), (V, k.init_done[1])], inc=P)
    pv = P.v
    ac = b['acols']
    k.op('dve', lambda h: h.tensor_copy(ac[:, 0:1], k.psum['b0'][:, 0:1]),
         wait=[(P, pv)], inc=V)
    k.drain('dve')
    k.op('dve', lambda h: h.tensor_scalar_mul(ac[:, 1:2], ac[:, 0:1], 0.5), inc=V)
    k.op('dve', lambda h: h.tensor_scalar_mul(ac[:, 2:3], ac[:, 0:1], -17.0), inc=V)
    k.drain('dve')
    k.op('dve', lambda h: h.tensor_mul(ac[:, 3:4], ac[:, 0:1], ac[:, 0:1]), inc=V)
    k.drain('dve')
    k.op('dve', lambda h: h.tensor_scalar_mul(ac[:, 3:4], ac[:, 3:4], 1.0 / 16.0), inc=V)
    k.op('dve', lambda h: h.tensor_sub(b['demb'][:], b['vemb1r'][:], b['vemb0r'][:]),
         inc=V)
    k.drain('dve')
    k.waitop('pe', [(k.sem('V'), k.sem('V').v)])
    k.attn_const_done = k.sem('V').v


def emit_attn_prologue(k, xTfull, win_q, win_k, kvec_hbm):
    b = k.bufs
    nc = k.nc
    P, A, V, S, W = k.sem('P'), k.sem('A'), k.sem('V'), k.sem('S'), k.sem('W')
    ac = b['acols']
    r_col, rh_col = ac[:, 0:1], ac[:, 1:2]
    s_col = ac[:, 3:4]
    bq, bk, bv = k.psum['b4'], k.psum['b5'], k.psum['b6']
    k.op('dve', lambda h: h.memset(b['q_raw'][:, 0:31], 0.0), inc=V)
    k.op('dve', lambda h: h.memset(b['q_raw'][:, SS - 1:SS], 0.0), inc=V)
    k.op('dve', lambda h: h.memset(b['kv_raw'][:, 0:31], 0.0), inc=V)
    k.op('dve', lambda h: h.memset(b['kv_raw'][:, SS - 1:SS], 0.0), inc=V)
    k.op('dve', lambda h: h.memset(b['vs_tm'][:], 0.0), inc=V)
    zv = V.v
    prev_evac = 0
    prev_pe = 0
    for tt in range(16):
        s0 = S.v
        for dc in range(16):
            k.op('sy', lambda h, d=dc, t=tt: h.dma_start(
                out=b['xblk'][:, d * TB:(d + 1) * TB],
                in_=xTfull[d * 128:(d + 1) * 128, t * TB:(t + 1) * TB]),
                wait=[(V, prev_evac)] if dc == 0 else (), inc=S, inc_n=16)
        k.waitop('dve', [(S, S.v)])
        emit_ln_gen(k, b['xblk'], b['ytile'], 1, TB, TB)
        vln = V.v
        for dc in range(16):
            w = [(V, vln), (V, k.attn_const_done)] if dc == 0 else ()
            k.op('pe', lambda h, d=dc, st=(dc == 0), sp=(dc == 15):
                 h.matmul(bq[:, :], b['wq_s'][:, d * 128:(d + 1) * 128],
                          b['ytile'][:, d * TB:(d + 1) * TB], start=st, stop=sp),
                 wait=w)
            k.op('pe', lambda h, d=dc, st=(dc == 0), sp=(dc == 15):
                 h.matmul(bk[0:32, :], b['wk_s'][:, d * 32:(d + 1) * 32],
                          b['ytile'][:, d * TB:(d + 1) * TB], start=st, stop=sp))
            k.op('pe', lambda h, d=dc, st=(dc == 0), sp=(dc == 15):
                 h.matmul(bv[0:32, :], b['wv_s'][:, d * 32:(d + 1) * 32],
                          b['ytile'][:, d * TB:(d + 1) * TB], start=st, stop=sp),
                 inc=(P if dc == 15 else None))
        pv = P.v
        col = 31 + tt * TB
        k.op('act', lambda h, c=col: h.activation(
            b['q_raw'][:, c:c + TB], bq[:, :], AF.Tanh, scale=r_col),
            wait=[(P, pv), (V, zv)], inc=A)
        k.op('act', lambda h, c=col: h.activation(
            b['kv_raw'][0:32, c:c + TB], bk[0:32, :], AF.Tanh, scale=r_col[0:32, :]),
            inc=A)
        k.op('act', lambda h, c=col: h.activation(
            b['ebuf'][0:32, 0:TB], bv[0:32, :], AF.Tanh, scale=rh_col[0:32, :]),
            inc=A)
        av = A.v
        k.op('dve', lambda h, c=col: h.tensor_scalar(
            b['kv_raw'][32:64, c:c + TB], b['ebuf'][0:32, 0:TB], 0.5, 0.5,
            ALU.mult, ALU.add), wait=[(A, av)], inc=V)
        k.drain('dve')
        k.op('dve', lambda h, c=col: h.tensor_scalar_mul(
            b['kv_raw'][0:32, c:c + TB], b['kv_raw'][0:32, c:c + TB],
            s_col[0:32, :]), inc=V)
        prev_evac = V.v
        prev_pe = P.v
    k.waitop('sy', [(V, prev_evac), (A, k.sem('A').v)])
    s0 = S.v
    for hh in range(16):
        for w in range(32):
            k.op('sy', lambda h, hh=hh, w=w: h.dma_start(
                out=win_q[256 * hh + 8 * w: 256 * hh + 8 * w + 8, 0:8192],
                in_=b['q_raw'][8 * hh: 8 * hh + 8, w: w + 8192]),
                inc=S, inc_n=16)
    for g in range(4):
        for w in range(32):
            k.op('sy', lambda h, g=g, w=w: h.dma_start(
                out=win_k[256 * g + 8 * w: 256 * g + 8 * w + 8, PADZ:PADZ + 8192],
                in_=b['kv_raw'][8 * g: 8 * g + 8, w: w + 8192]),
                inc=S, inc_n=16)
    for i in range(8):
        for q in range(4):
            k.op('sy', lambda h, i=i, q=q: h.dma_start(
                out=win_k[i * 128:(i + 1) * 128, q * 2048:(q + 1) * 2048],
                in_=bass_zero_src(k)), inc=S, inc_n=16)
    k.win_done = S.v
    vc = b['vcop']
    for kt in range(64):
        k.op('dve', lambda h, kt=kt: h.tensor_copy(
            vc[:], b['kv_raw'][32:64, 32 + kt * 128: 32 + (kt + 1) * 128]),
            wait=[(V, prev_evac)] if kt == 0 else (), inc=V)
        vv = V.v
        k.op('pe', lambda h: h.transpose(k.psum['b7'][:, 0:32], vc[:],
                                         b['id_f32'][0:32, 0:32]),
             wait=[(V, vv)], inc=P)
        pv = P.v
        for g in range(4):
            k.op('dve', lambda h, kt=kt, g=g: h.tensor_copy(
                b['vs_tm'][:, (64 + kt) * 36 + g * 9 + 1: (64 + kt) * 36 + g * 9 + 9],
                k.psum['b7'][:, g * 8:(g + 1) * 8]),
                wait=[(P, pv)] if g == 0 else (), inc=V)
        k.drain('dve')
    for kt in range(64):
        for g in range(4):
            k.op('dve', lambda h, kt=kt, g=g: h.memset(
                b['vs_tm'][:, (64 + kt) * 36 + g * 9: (64 + kt) * 36 + g * 9 + 1], 1.0),
                inc=V)
    vsv = k.sem('V').v
    k.op('sy', lambda h: h.dma_start(out=k.vs_stage, in_=b['vs_tm']),
         wait=[(V, vsv)], inc=S, inc_n=16)
    k.vs_done = S.v


def bass_zero_src(k):
    return k.bufs['vs_tm'][:, 0:2048]


def emit_attn_main(k, coreinfo_sb, win_q, win_k, kvec_hbm, qp1_hbm):
    b = k.bufs
    nc = k.nc
    P, A, V, S, W, GA = (k.sem('P'), k.sem('A'), k.sem('V'), k.sem('S'),
                         k.sem('W'), k.sem('GA'))
    if not hasattr(k, 'gpregs'):
        k.gpregs = [nc.sync.alloc_register(f'symofs{i}') for i in range(4)]
    ac = b['acols']
    m17_col = ac[:, 2:3]
    sb = [k.psum['b0'], k.psum['b1'], k.psum['b2'], k.psum['b3']]
    pvb = k.psum['b4']
    epb = k.psum['b5']
    ebuf = b['ebuf']
    qp1sb = b['qp1sb']
    k.op('sy', lambda h: h.dma_start(out=qp1sb, in_=qp1_hbm), inc=S, inc_n=16)
    sv = S.v
    tt = b['tinyf']
    k.op('dve', lambda h: h.tensor_scalar_mul(qp1sb, qp1sb, tt[0:1, 0:1]),
         wait=[(S, sv), (V, k.attn_const_done)], inc=V)
    k.drain('dve')
    k.op('dve', lambda h: h.reciprocal(b['crow_f'], qp1sb), inc=V)
    k.drain('dve')
    k.op('dve', lambda h: h.tensor_copy(b['crow'], b['crow_f']), inc=V)
    k.drain('dve')
    crow_done = V.v

    LOCK_COL = [0, 16 * 128, 48 * 128, 96 * 128]
    VS_COL = [0, 16 * 36, 48 * 36, 96 * 36]
    KV_COL = [0, 16 * 128, 48 * 128, 96 * 128]
    rwait = [(S, max(k.win_done, k.vs_done))]
    for blk in range(4):
        padK = PADK[blk]
        def rq(h, blk=blk):
            h.reg_load(k.gpregs[0], coreinfo_sb[0:1, blk:blk + 1])
            return h.dma_start(
                out=k.loc_q[:, blk * 512:(blk + 1) * 512],
                in_=AP(tensor=win_q.tensor, offset=k.gpregs[0],
                       ap=[[WQ, 4096], [1, 512]]))
        k.op('sy', rq, wait=rwait if blk == 0 else (), inc=S, inc_n=16)
        def rk(h, blk=blk, padK=padK):
            h.reg_load(k.gpregs[1], coreinfo_sb[0:1, 4 + blk:5 + blk])
            return h.dma_start(
                out=k.loc_k[:, LOCK_COL[blk]:LOCK_COL[blk] + padK * 128],
                in_=AP(tensor=win_k.tensor, offset=k.gpregs[1],
                       ap=[[WK, 1024], [1, padK * 128]]))
        k.op('sy', rk, inc=S, inc_n=16)
        def rv(h, blk=blk, padK=padK):
            h.reg_load(k.gpregs[2], coreinfo_sb[0:1, 8 + blk:9 + blk])
            return h.dma_start(
                out=k.loc_vs[:, VS_COL[blk]:VS_COL[blk] + padK * 36],
                in_=AP(tensor=k.vs_stage.tensor, offset=k.gpregs[2],
                       ap=[[128 * 36, 128], [1, padK * 36]]))
        k.op('sy', rv, inc=S, inc_n=16)
        def rkv(h, blk=blk, padK=padK):
            h.reg_load(k.gpregs[3], coreinfo_sb[0:1, 12 + blk:13 + blk])
            return h.dma_start(
                out=k.loc_kv[0:1, KV_COL[blk]:KV_COL[blk] + padK * 128],
                in_=AP(tensor=kvec_hbm.tensor, offset=k.gpregs[3],
                       ap=[[1, padK * 128]]))
        k.op('sy', rkv, inc=S, inc_n=16)
    restage_done = S.v

    for g in range(4):
        for blk in range(4):
            padK = PADK[blk]
            gw = [(S, restage_done), (P, getattr(k, 'kw_guard', 0))]
            for c in range(2):
                k.op('gp', lambda h, g=g, blk=blk, c=c, padK=padK: h.dma_start(
                    out=b['kwbuf'][:, c * 8192: c * 8192 + padK * 128],
                    in_=k.loc_k[g * 256 + 128 * c: g * 256 + 128 * (c + 1),
                                LOCK_COL[blk]:LOCK_COL[blk] + padK * 128]),
                    wait=gw if c == 0 else (), inc=W, inc_n=16)
            for j in range(4):
                hh = g * 4 + j
                for c in range(2):
                    k.op('gp', lambda h, hh=hh, c=c, blk=blk, j=j: h.dma_start(
                        out=b['qwbuf'][:, (j * 2 + c) * TB: (j * 2 + c + 1) * TB],
                        in_=k.loc_q[256 * hh + 128 * c: 256 * hh + 128 * (c + 1),
                                    blk * 512:(blk + 1) * 512]),
                        inc=W, inc_n=16)
            k.op('gp', lambda h, padK=padK, blk=blk: h.dma_start(
                out=b['vsblk'][:, 0:padK * 36],
                in_=k.loc_vs[:, VS_COL[blk]:VS_COL[blk] + padK * 36]),
                inc=W, inc_n=16)
            k.op('gp', lambda h, padK=padK, blk=blk: h.dma_start(
                out=b['kvec_host'][0:1, 0:padK * 128],
                in_=k.loc_kv[0:1, KV_COL[blk]:KV_COL[blk] + padK * 128]),
                inc=W, inc_n=16)
            wv = W.v
            exp_marks = [0] * 4
            last_pv = 0
            for kt in range(padK):
                diag_t = kt - (padK - 4)
                ga_marks = [0] * 4
                for j in range(4):
                    pw = []
                    if kt == 0 and j == 0:
                        pw = [(W, wv), (V, crow_done)]
                    if kt >= 1:
                        pw = pw + [(A, exp_marks[j])]
                    k.op('pe', lambda h, j=j, kt=kt: h.matmul(
                        sb[j][:, :], b['kwbuf'][:, kt * 128:(kt + 1) * 128],
                        b['qwbuf'][:, (j * 2) * TB:(j * 2 + 1) * TB],
                        start=True, stop=False), wait=pw)
                    k.op('pe', lambda h, j=j, kt=kt: h.matmul(
                        sb[j][:, :], b['kwbuf'][:, 8192 + kt * 128: 8192 + (kt + 1) * 128],
                        b['qwbuf'][:, (j * 2 + 1) * TB:(j * 2 + 2) * TB],
                        start=False, stop=False))
                    k.op('pe', lambda h, j=j, kt=kt, blk=blk: h.matmul(
                        sb[j][:, :], b['kvec_sb'][0:1, kt * 128:(kt + 1) * 128],
                        b['crow'][0:1, blk * TB:(blk + 1) * TB],
                        start=False, stop=True), inc=P)
                    pvmm = P.v
                    k.op('act', lambda h, j=j: h.activation(
                        ebuf[:, j * TB:(j + 1) * TB], sb[j][:, :], AF.Exp,
                        bias=m17_col),
                        wait=[(P, pvmm), (P, last_pv)], inc=A)
                    exp_marks[j] = A.v
                    if diag_t >= 0:
                        k.op('gp', lambda h, j=j, dt=diag_t: h.affine_select(
                            ebuf[:, j * TB:(j + 1) * TB],
                            ebuf[:, j * TB:(j + 1) * TB],
                            pattern=[[1, TB]], compare_op=ALU.is_gt, fill=0.0,
                            base=-128 * dt, channel_multiplier=-1),
                            wait=[(A, exp_marks[j])], inc=GA)
                        ga_marks[j] = k.sem('GA').v
                for j in range(4):
                    pw = [(A, exp_marks[j])]
                    if diag_t >= 0:
                        pw = [(GA, ga_marks[j])]
                    if kt == 0 and j == 0:
                        pw = pw + [(V, getattr(k, 'pv_guard', 0))]
                    k.op('pe', lambda h, j=j, kt=kt, g=g, padK=padK: h.matmul(
                        pvb[32 * j: 32 * j + 9, :],
                        b['vsblk'][:, kt * 36 + g * 9: kt * 36 + (g + 1) * 9],
                        ebuf[:, j * TB:(j + 1) * TB],
                        start=(kt == 0), stop=(kt == padK - 1),
                        tile_position=(0, 32 * j),
                        skip_group_check=(j > 0)), wait=pw,
                        inc=(P if j == 3 else None))
                last_pv = P.v
            k.kw_guard = P.v
            pvdone = P.v
            k.op('dve', lambda h: h.tensor_copy(b['pv_sb'][:], pvb[:, :]),
                 wait=[(P, pvdone)], inc=V)
            k.drain('dve')
            rows = b['rows']
            for j in range(4):
                rm = rows[:, j * TB:(j + 1) * TB]
                k.op('dve', lambda h, j=j, rm=rm: h.tensor_scalar_max(
                    rm, b['pv_sb'][32 * j: 32 * j + 1, :], 1e-30), inc=V)
            k.drain('dve')
            for j in range(4):
                rm = rows[:, j * TB:(j + 1) * TB]
                k.op('dve', lambda h, rm=rm: h.reciprocal(rm, rm), inc=V)
            k.drain('dve')
            vv = V.v
            for j in range(4):
                rm = rows[:, j * TB:(j + 1) * TB]
                k.op('pe', lambda h, j=j, rm=rm: h.matmul(
                    epb[32 * j: 32 * j + 32, :], b['ones_row_f32'][0:1, 0:32],
                    rm, start=True, stop=True, tile_position=(0, 32 * j),
                    skip_group_check=(j > 0)),
                    wait=[(V, vv)] if j == 0 else (), inc=(P if j == 3 else None))
            pv2 = P.v
            ocol = (blk * 4 + g) * TB
            for j in range(4):
                k.op('dve', lambda h, j=j, ocol=ocol: h.tensor_mul(
                    b['outat'][32 * j:32 * j + 32, ocol:ocol + TB],
                    b['pv_sb'][32 * j:32 * j + 32, :],
                    epb[32 * j:32 * j + 32, :]),
                    wait=[(P, pv2)] if j == 0 else (), inc=V)
            k.drain('dve')
            for j in range(4):
                k.op('dve', lambda h, j=j, ocol=ocol, g=g: h.tensor_scalar(
                    b['outat'][32 * j:32 * j + 32, ocol:ocol + TB],
                    b['outat'][32 * j:32 * j + 32, ocol:ocol + TB],
                    b['demb'][32 * j:32 * j + 32, g:g + 1],
                    b['vemb0r'][32 * j:32 * j + 32, g:g + 1],
                    ALU.mult, ALU.add), inc=V)
            k.drain('dve')
            k.pv_guard = V.v


def emit_attn_final(k, xin, outx):
    b = k.bufs
    P, V, S = k.sem('P'), k.sem('V'), k.sem('S')
    SO = k.sem('SO')
    att_done = V.v
    k.xadd_marks = [0] * 16
    for blk in range(4):
        s0 = S.v
        for dc in range(16):
            w0 = [(SO, k.sem('SO').v), (V, att_done)] if dc == 0 else ()
            k.op('sy', lambda h, d=dc, blk=blk: h.dma_start(
                out=b['xblk'][:, d * TB:(d + 1) * TB],
                in_=xin[d * 128:(d + 1) * 128, blk * TB:(blk + 1) * TB]),
                wait=w0, inc=S, inc_n=16)
        sx = S.v
        for dc in range(16):
            par = dc % 2
            bank = k.psum['b6'] if par == 0 else k.psum['b7']
            pw = [(V, att_done), (S, k.sem('S').v)] if dc == 0 else ()
            if dc >= 2:
                pw = [(V, k.xadd_marks[dc - 2])]
            for g in range(4):
                k.op('pe', lambda h, d=dc, g=g, blk=blk, bank=bank: h.matmul(
                    bank[:, :], b['wo_s'][:, g * 2048 + d * 128: g * 2048 + (d + 1) * 128],
                    b['outat'][:, (blk * 4 + g) * TB:(blk * 4 + g + 1) * TB],
                    start=(g == 0), stop=(g == 3)),
                    wait=pw if g == 0 else (), inc=(P if g == 3 else None))
            pv = P.v
            k.op('dve', lambda h, d=dc, bank=bank: h.tensor_add(
                b['xblk'][:, d * TB:(d + 1) * TB],
                b['xblk'][:, d * TB:(d + 1) * TB], bank[:, :]),
                wait=[(P, pv)], inc=V)
            k.xadd_marks[dc] = V.v
        k.waitop('sy', [(V, k.sem('V').v)])
        for dc in range(16):
            k.op('sy', lambda h, d=dc, blk=blk: h.dma_start(
                out=outx[d * 128:(d + 1) * 128, blk * TB:(blk + 1) * TB],
                in_=b['xblk'][:, d * TB:(d + 1) * TB]),
                inc=SO, inc_n=16)


def emit_ln_gen(k, xh, yT, n_tb, pitch_x, pitch_y, ytb0=0):
    b = k.bufs
    P, A, V = k.sem('P'), k.sem('A'), k.sem('V')
    bsum, bsq, ba, bb = k.psum['b0'], k.psum['b1'], k.psum['b2'], k.psum['b3']
    self = k
    self.waitop('dve', [(P, P.v)])
    for tb in range(n_tb):
        t0 = tb * TB
        pe_marks = []
        v0 = V.v
        a0 = A.v
        for dc in range(16):
            par = dc % 2
            xsl = xh[:, dc * pitch_x + t0: dc * pitch_x + t0 + TB]
            xb = b['ln_xbf'][:, par * TB:(par + 1) * TB]
            sq = b['ln_sq'][:, par * TB:(par + 1) * TB]
            w = [(P, pe_marks[dc - 2])] if dc >= 2 else []
            self.op('dve', lambda h, o=xb, i=xsl: h.tensor_copy(o, i), wait=w, inc=V)
            aw = [(V, v0 + dc + 1)]
            if dc >= 2:
                aw.append((P, pe_marks[dc - 2]))
            self.op('act', lambda h, o=sq, i=xb: h.activation(o, i, AF.Square),
                    wait=aw, inc=A)
            iw = [(V, v0 + dc + 1)]
            if dc == 0:
                iw.append((V, self.init_done[1]))
            self.op('pe', lambda h, i=xb, st=(dc == 0), sp=(dc == 15):
                    h.matmul(bsum[0:1, :], b['ones_col_bf'][:, 0:1], i,
                             start=st, stop=sp), wait=iw)
            self.op('pe', lambda h, i=sq, st=(dc == 0), sp=(dc == 15):
                    h.matmul(bsq[0:1, :], b['ones_col_bf'][:, 0:1], i,
                             start=st, stop=sp),
                    wait=[(A, a0 + dc + 1)], inc=P)
            pe_marks.append(P.v)
        r = b['rows']
        mu, ex2 = r[:, 0:TB], r[:, TB:2 * TB]
        var, rstd, brow = r[:, 2 * TB:3 * TB], r[:, 3 * TB:4 * TB], r[:, 4 * TB:5 * TB]
        self.op('dve', lambda h: h.tensor_scalar_mul(mu, bsum[0:1, :], 1.0 / 2048),
                wait=[(P, P.v)], inc=V)
        self.op('dve', lambda h: h.tensor_scalar_mul(ex2, bsq[0:1, :], 1.0 / 2048), inc=V)
        self.drain('dve')
        self.op('dve', lambda h: h.tensor_mul(var, mu, mu), inc=V)
        self.drain('dve')
        self.op('dve', lambda h: h.tensor_sub(var, ex2, var), inc=V)
        self.op('act', lambda h: h.activation(var, var, AF.Ln, bias=float(EPS)),
                wait=[(V, V.v)], inc=A)
        self.drain('act')
        self.op('act', lambda h: h.activation(rstd, var, AF.Exp, scale=-0.5), inc=A)
        self.drain('dve')
        self.op('dve', lambda h: h.scalar_tensor_tensor(brow, mu, -1.0, rstd,
                                                        ALU.mult, ALU.mult),
                wait=[(A, A.v)], inc=V)
        self.op('pe', lambda h: h.matmul(ba[:, :], b['ones_row_f32'][0:1, :],
                                         rstd, start=True, stop=True),
                wait=[(V, V.v)])
        self.op('pe', lambda h: h.matmul(bb[:, :], b['ones_row_f32'][0:1, :],
                                         brow, start=True, stop=True), inc=P)
        pv = P.v
        for dc0 in range(0, 16, 2):
            for j in range(2):
                dc = dc0 + j
                xsl = xh[:, dc * pitch_x + t0: dc * pitch_x + t0 + TB]
                self.op('dve', lambda h, x=xsl, j=j: h.tensor_mul(
                    b['ln_tmp'][:, j * TB:(j + 1) * TB], x, ba[:, :]),
                    wait=[(P, pv)] if dc == 0 else (), inc=V)
            self.drain('dve')
            for j in range(2):
                dc = dc0 + j
                ysl = yT[:, dc * pitch_y + (ytb0 + tb) * TB:
                         dc * pitch_y + (ytb0 + tb) * TB + TB]
                self.op('dve', lambda h, y=ysl, j=j: h.tensor_add(
                    y, b['ln_tmp'][:, j * TB:(j + 1) * TB], bb[:, :]), inc=V)
            self.drain('dve')
        self.waitop('pe', [(V, V.v)])


def emit_gram(k, xhbm, epart_out):
    b = k.bufs
    P, A, V, S = k.sem('P'), k.sem('A'), k.sem('V'), k.sem('S')
    SO = k.sem('SO')
    for th in range(2):
        s0 = S.v
        for dc in range(16):
            k.op('sy', lambda h, d=dc, t=th: h.dma_start(
                out=b['xh'][:, d * TH:(d + 1) * TH],
                in_=xhbm[d * 128:(d + 1) * 128, t * TH:(t + 1) * TH]),
                wait=[(V, k.gram_xh_guard)] if (dc == 0 and th > 0) else (),
                inc=S, inc_n=16)
        k.waitop('dve', [(S, S.v)])
        emit_ln_gen(k, b['xh'], b['ytmp'], 2, TH, TH, 0)
        k.gram_xh_guard = k.sem('V').v
        for tloc in range(8):
            tt = th * 8 + tloc
            for dc in range(16):
                k.op('dve', lambda h, tl=tloc, d=dc: h.tensor_copy(
                    b['yfc'][:, :],
                    b['ytmp'][:, d * TH + tl * 128:d * TH + (tl + 1) * 128]),
                    wait=[(P, k.yfc_guard)] if k.yfc_guard > 0 else (), inc=V)
                vv = V.v
                k.op('pe', lambda h: h.transpose(
                    k.psum['b4'][:, 0:128], b['yfc'][:, :], b['id_f32'][:, :]),
                    wait=[(V, vv)], inc=P)
                pv = P.v
                k.yfc_guard = pv
                k.op('dve', lambda h, t_=tt, d=dc: h.tensor_copy(
                    b['y_tm'][:, (t_ * 16 + d) * 128:(t_ * 16 + d + 1) * 128],
                    k.psum['b4'][:, 0:128]),
                    wait=[(P, pv)], inc=V)
                k.drain('dve')
    vfin = k.sem('V').v
    vmarks = [0, 0]
    somarks = [0, 0]
    first = True
    for ic in range(16):
        for jb in range(4):
            par = (ic * 4 + jb) % 2
            bank = k.psum['b5'] if par == 0 else k.psum['b6']
            for ct in range(16):
                w = []
                if first and ct == 0:
                    w = [(V, vfin)]
                    first = False
                elif ct == 0 and vmarks[par] > 0:
                    w = [(V, vmarks[par])]
                k.op('pe', lambda h, ic=ic, jb=jb, ct=ct, bank=bank: h.matmul(
                    bank[:, :],
                    b['y_tm'][:, (ct * 16 + ic) * 128:(ct * 16 + ic + 1) * 128],
                    b['y_tm'][:, ct * 2048 + jb * 512: ct * 2048 + jb * 512 + 512],
                    start=(ct == 0), stop=(ct == 15)), wait=w,
                    inc=(P if ct == 15 else None))
            pv = P.v
            ew = [(P, pv)]
            if somarks[par] > 0:
                ew.append((SO, somarks[par]))
            k.op('dve', lambda h, bank=bank, par=par: h.tensor_copy(
                b['egr'][:, par * TB:(par + 1) * TB], bank[:, :]),
                wait=ew, inc=V)
            vmarks[par] = k.sem('V').v
            vv = k.sem('V').v
            k.op('sy', lambda h, ic=ic, jb=jb, par=par: h.dma_start(
                out=epart_out[ic * 128:(ic + 1) * 128, jb * TB:(jb + 1) * TB],
                in_=b['egr'][:, par * TB:(par + 1) * TB]),
                wait=[(V, vv)], inc=SO, inc_n=16)
            somarks[par] = k.sem('SO').v


def emit_attnb_post(k, eparts, esum_hbm, x2in, outx):
    b = k.bufs
    P, A, V, S = k.sem('P'), k.sem('A'), k.sem('V'), k.sem('S')
    SO = k.sem('SO')
    ep4 = b['xh']
    Et = b['Et']; Et2 = b['Et2']; ebig = b['ebig']
    so_guard = 0
    for it in range(16):
        sv0 = S.v
        for p_ in range(4):
            w = []
            if p_ == 0 and it > 0:
                w = [(V, k.p1_guard)]
            if p_ == 0 and so_guard > 0:
                w = w + [(SO, so_guard)]
            k.op('sy', lambda h, it=it, p_=p_: h.dma_start(
                out=ep4[:, p_ * 2048:(p_ + 1) * 2048],
                in_=eparts[p_][it * 128:(it + 1) * 128, :]),
                wait=w, inc=S, inc_n=16)
        sv = S.v
        k.op('dve', lambda h: h.tensor_add(Et, ep4[:, 0:2048], ep4[:, 2048:4096]),
             wait=[(S, sv)], inc=V)
        k.op('dve', lambda h: h.tensor_add(Et2, ep4[:, 4096:6144], ep4[:, 6144:8192]),
             inc=V)
        k.drain('dve')
        k.op('dve', lambda h: h.tensor_add(Et, Et, Et2), inc=V)
        k.drain('dve')
        k.op('dve', lambda h, it=it: h.reduce_max(
            b['mpcols'][:, it:it + 1], Et, axis=AX, negate=True), inc=V)
        k.drain('dve')
        k.op('act', lambda h, it=it: h.activation(
            ebig, Et, AF.Exp, bias=b['mpcols'][:, it:it + 1]),
            wait=[(V, V.v)], inc=A)
        k.op('dve', lambda h: h.reduce_sum(b['scol'], ebig, axis=AX),
             wait=[(A, A.v)], inc=V)
        k.drain('dve')
        k.op('act', lambda h: h.activation(b['scol2'], b['scol'], AF.Ln),
             wait=[(V, V.v)], inc=A)
        k.op('dve', lambda h, it=it: h.tensor_sub(
            b['mpcols'][:, it:it + 1], b['scol2'], b['mpcols'][:, it:it + 1]),
            wait=[(A, A.v)], inc=V)
        k.p1_guard = k.sem('V').v
        k.op('sy', lambda h, it=it: h.dma_start(
            out=esum_hbm[it * 128:(it + 1) * 128, :], in_=Et),
            wait=[(V, k.p1_guard)], inc=SO, inc_n=16)
        so_guard = k.sem('SO').v
    for it in range(16):
        vv = V.v
        k.op('pe', lambda h, it=it: h.transpose(
            k.psum['b7'][0:1, 0:128], b['mpcols'][:, it:it + 1], b['id_f32'][:, :]),
            wait=[(V, vv)], inc=P)
        pv = P.v
        k.op('dve', lambda h, it=it: h.tensor_copy(
            b['mprow'][0:1, it * 128:(it + 1) * 128], k.psum['b7'][0:1, 0:128]),
            wait=[(P, pv)], inc=V)
    k.drain('dve')
    for th in range(2):
        for dc in range(16):
            w = [(V, k.sem('V').v)] if (dc == 0) else ()
            k.op('sy', lambda h, d=dc, t=th: h.dma_start(
                out=b['xh'][:, d * TH:(d + 1) * TH],
                in_=x2in[d * 128:(d + 1) * 128, t * TH:(t + 1) * TH]),
                wait=w, inc=S, inc_n=16)
        k.waitop('dve', [(S, S.v)])
        emit_ln_gen(k, b['xh'], b['y_full'], 2, TH, T, ytb0=2 * th)
    k.waitop('sy', [(SO, SO.v)])
    yv = k.sem('V').v
    banks = [k.psum['b0'], k.psum['b1'], k.psum['b2'], k.psum['b3']]
    ab_guard = [0, 0]
    so2 = 0
    SXR = k.sem('SXR')
    SE0, SE1 = k.sem('SE0'), k.sem('SE1')
    SEs = [SE0, SE1]
    for ic in range(16):
        xw = [(V, k.p3_guard), (SO, SO.v)] if k.p3_guard > 0 else [(V, yv)]
        k.op('sy', lambda h, ic=ic: h.dma_start(
            out=b['xrow'], in_=x2in[ic * 128:(ic + 1) * 128, :]),
            wait=xw, inc=SXR, inc_n=16)
        sx = SXR.v
        k.op('pe', lambda h, ic=ic: h.matmul(
            k.psum['b6'][:, 0:128], b['ones_row_f32'][0:1, :],
            b['mprow'][0:1, ic * 128:(ic + 1) * 128], start=True, stop=True),
            wait=[(V, yv), (V, k.p3_mbc)] if k.p3_mbc > 0 else [(V, yv)], inc=P)
        mb_pv = P.v
        for jt in range(16):
            par = jt % 2
            ew = [(V, k.sub_marks[par])] if k.sub_marks[par] > 0 else []
            k.op('sy', lambda h, ic=ic, jt=jt, par=par: h.dma_start(
                out=b['etile'][:, par * 128:(par + 1) * 128],
                in_=AP(tensor=esum_hbm.tensor,
                       offset=(jt * 128) * 2048 + ic * 128,
                       ap=[[2048, 128], [1, 128]])),
                wait=ew, inc=SEs[par], inc_n=16)
            se = SEs[par].v
            dw = [(SEs[par], se), (P, mb_pv)]
            if k.exp_marks[par] > 0:
                dw.append((A, k.exp_marks[par]))
            k.op('dve', lambda h, par=par: h.tensor_sub(
                b['asub'][:, par * 128:(par + 1) * 128],
                b['etile'][:, par * 128:(par + 1) * 128], k.psum['b6'][:, 0:128]),
                wait=dw, inc=V)
            k.sub_marks[par] = k.sem('V').v
            aw = [(V, k.sub_marks[par])]
            if k.mm_marks[par] > 0:
                aw.append((P, k.mm_marks[par]))
            k.op('act', lambda h, par=par: h.activation(
                b['abuf'][:, par * 128:(par + 1) * 128],
                b['asub'][:, par * 128:(par + 1) * 128], AF.Exp),
                wait=aw, inc=A)
            av = A.v
            k.exp_marks[par] = av
            for cb in range(4):
                k.op('pe', lambda h, ic=ic, jt=jt, cb=cb, par=par: h.matmul(
                    banks[cb][:, :], b['abuf'][:, par * 128:(par + 1) * 128],
                    b['y_full'][:, jt * T + cb * TB: jt * T + (cb + 1) * TB],
                    start=(jt == 0), stop=(jt == 15)),
                    wait=[(A, av)] if cb == 0 else (),
                    inc=(P if cb == 3 else None))
            k.mm_marks[par] = k.sem('P').v
        pv = P.v
        for cb in range(4):
            k.op('dve', lambda h, cb=cb: h.tensor_add(
                b['xrow'][:, cb * TB:(cb + 1) * TB],
                b['xrow'][:, cb * TB:(cb + 1) * TB], banks[cb][:, :]),
                wait=[(P, pv), (SXR, sx)] if cb == 0 else (), inc=V)
        k.p3_guard = k.sem('V').v
        k.p3_mbc = k.p3_guard
        k.op('sy', lambda h, ic=ic: h.dma_start(
            out=outx[ic * 128:(ic + 1) * 128, :], in_=b['xrow']),
            wait=[(V, k.p3_guard)], inc=SO, inc_n=16)



def _common_small(k, tens):
    b = k.bufs
    S, V = k.sem('S'), k.sem('V')
    k.emit_const_init(tens['sel8c'], tens['idc'])
    k.op('dve', lambda h: h.tensor_copy(b['id_bf'], b['id_f32']),
         wait=[(S, k.init_done[0])], inc=V)
    k.init_done = (k.init_done[0], V.v)


def alloc_small(k):
    nc = k.nc
    al = nc.alloc_sbuf_tensor
    b = k.bufs
    b['ones_col_bf'] = al('ones_col_bf', [128, 1], BF16)
    b['ones_row_f32'] = al('ones_row_f32', [1, 128], F32)
    b['id_f32'] = al('id_f32', [128, 128], F32)[:]
    b['id_bf'] = al('id_bf', [128, 128], BF16)[:]
    b['ln_xbf'] = al('ln_xbf', [128, 2 * TB], BF16)
    b['ln_sq'] = al('ln_sq', [128, 2 * TB], BF16)
    b['ln_tmp'] = al('ln_tmp', [128, 2 * TB], F32)
    b['rows'] = al('rows', [1, 5 * TB], F32)
    b['sel8'] = al('sel8', [8, 8 * 128], F32)
    ps = nc.alloc_psum_tensor
    for i in range(8):
        k.psum[f'b{i}'] = ps(f'psb{i}', [128, TB], F32)


def build_attn(nc, t):
    k = K(nc)
    alloc_small(k)
    al = nc.alloc_sbuf_tensor
    b = k.bufs
    b['acols'] = al('acols', [128, 8], F32)[:]
    b['demb'] = al('demb', [128, 4], F32)[:]
    b['vemb0r'] = al('vemb0r', [128, 4], F32)[:]
    b['vemb1r'] = al('vemb1r', [128, 4], F32)[:]
    b['tinyf'] = al('tinyf', [1, 8], F32)[:]
    b['vcop'] = al('vcop', [32, 128], F32)[:]
    b['cinfo'] = al('cinfo', [1, 256], I32)[:]
    b['q_raw'] = al('q_raw', [128, SS], BF16)[:]
    b['kv_raw'] = al('kv_raw', [64, SS], BF16)[:]
    b['vs_tm'] = al('vs_tm', [128, 128 * 36], BF16)[:]
    b['kwbuf'] = al('kwbuf', [128, 2 * 8192], BF16)[:]
    b['qwbuf'] = al('qwbuf', [128, 8 * TB], BF16)[:]
    b['ebuf'] = al('ebuf', [128, 4 * TB], BF16)[:]
    b['vsblk'] = al('vsblk', [128, 64 * 36], BF16)[:]
    b['outat'] = al('outat', [128, 16 * TB], BF16)[:]
    b['xblk'] = al('xblk', [128, 16 * TB], F32)[:]
    b['ytile'] = b['outat']
    b['crow'] = al('crow', [1, 4 * TB], BF16)[:]
    b['crow_f'] = b['rows'][:, 0:2048]
    b['qp1sb'] = b['rows'][:, 2048:2560]
    b['kvec_host'] = b['q_raw']
    b['kvec_sb'] = b['q_raw'][0:1, 0:8192]
    b['wq_s'] = al('wq_s', [128, 2048], BF16)[:]
    b['wk_s'] = al('wk_s', [128, 512], BF16)[:]
    b['wv_s'] = al('wv_s', [128, 512], BF16)[:]
    b['wo_s'] = al('wo_s', [128, 8192], BF16)[:]
    b['pv_sb'] = al('pv_sb', [128, TB], F32)[:]
    b['qp1sb'] = al('qp1sb', [1, 2048], F32)[:]
    b['crow_f'] = al('crow_f', [1, 2048], F32)[:]
    k.vs_stage = t['vs_stage']
    k.loc_q, k.loc_k = t['loc_q'], t['loc_k']
    k.loc_vs, k.loc_kv = t['loc_vs'], t['loc_kv']
    _common_small(k, t)
    S, W, V, P = k.sem('S'), k.sem('W'), k.sem('V'), k.sem('P')
    k.op('gp', lambda h: h.dma_start(out=b['wq_s'], in_=t['wq_h']), inc=W, inc_n=16)
    k.op('gp', lambda h: h.dma_start(out=b['wk_s'], in_=t['wk_h']), inc=W, inc_n=16)
    k.op('gp', lambda h: h.dma_start(out=b['wv_s'], in_=t['wv_h']), inc=W, inc_n=16)
    k.op('gp', lambda h: h.dma_start(out=b['wo_s'], in_=t['wo_h']), inc=W, inc_n=16)
    k.op('sy', lambda h: h.dma_start(out=b['cinfo'], in_=t['ci']), inc=S, inc_n=16)
    k.op('sy', lambda h: h.dma_start(out=b['vemb0r'], in_=t['v0r']), inc=S, inc_n=16)
    k.op('sy', lambda h: h.dma_start(out=b['vemb1r'], in_=t['v1r']), inc=S, inc_n=16)
    k.waitop('dve', [(S, S.v), (W, W.v)])
    k.waitop('pe', [(S, S.v), (W, W.v)])
    emit_attn_consts(k, t['tauh'])
    emit_attn_prologue(k, t['xtf'], t['win_q'], t['win_k'], t['kvec'])
    emit_attn_main(k, b['cinfo'], t['win_q'], t['win_k'], t['kvec'], t['qp1'])
    emit_attn_final(k, t['xin'], t['outx'])
    SO = k.sem('SO')
    k.waitop('sy', [(SO, SO.v)])
    k.waitop('gp', [(SO, SO.v)])
    with nc.Block() as blk:
        k.finalize(blk)
    return k


def build_ffn(nc, t):
    k = K(nc)
    alloc_small(k)
    al = nc.alloc_sbuf_tensor
    b = k.bufs
    b['lt'] = al('lt', [128, 8], F32)
    b['lte'] = al('lte', [128, 8], F32)
    b['ltp'] = al('ltp', [128, 8], F32)
    b['ltm'] = al('ltm', [128, 8], F32)
    b['ltw'] = al('ltw', [128, 8], F32)
    b['cols'] = al('cols', [128, 8], F32)
    b['wslab'] = [al('wslab0', [128, 2048], BF16), al('wslab1', [128, 2048], BF16)]
    b['dslab'] = [al('dslab0', [128, 1408], BF16), al('dslab1', [128, 1408], BF16)]
    b['tmp_silu'] = al('tmp_silu', [128, 4 * TB], BF16)
    b['h'] = al('h', [128, 11 * TH], BF16)
    b['sig_bc'] = al('sig_bc', [128, TH], BF16)
    b['we_bc'] = al('we_bc', [128, TH], BF16)
    b['w_rows'] = al('w_rows', [8, TH], F32)
    b['rl'] = al('rl', [9, TH], F32)
    b['xh'] = al('xh', [128, 16 * TH], F32)[:]
    b['yT'] = al('yT', [128, 16 * TH], BF16)[:]
    _common_small(k, t)
    S, SO2, V = k.sem('S'), k.sem('SOF'), k.sem('V')
    for th in range(2):
        for dc in range(16):
            w = [(SO2, SO2.v)] if dc == 0 else ()
            k.op('sy', lambda h, d=dc, t_=th: h.dma_start(
                out=b['xh'][:, d * TH:(d + 1) * TH],
                in_=t['xin'][d * 128:(d + 1) * 128, t_ * TH:(t_ + 1) * TH]),
                wait=w, inc=S, inc_n=16)
        k.waitop('dve', [(S, S.v)])
        emit_ln_gen(k, b['xh'], b['yT'], 2, TH, TH)
        k.emit_router(b['yT'], t['rts'], th)
        hg = 0
        for q in range(4):
            gu_slabs = [t['gus'][22 * q + i] for i in range(22)]
            down_slabs = [t['dws'][d][:, 11 * q * 128:(11 * q + 11) * 128]
                          for d in range(16)]
            hg = k.emit_quarter(b['xh'], b['yT'], gu_slabs, down_slabs, 11,
                                b['sig_bc'], hg)
        for e in range(8):
            k.emit_we_bcast(e)
            gu_slabs = [t['e13'][22 * e + i] for i in range(22)]
            down_slabs = [t['e2'][16 * e + d] for d in range(16)]
            hg = k.emit_quarter(b['xh'], b['yT'], gu_slabs, down_slabs, 11,
                                b['we_bc'], hg)
        k.waitop('sy', [(V, V.v)])
        for dc in range(16):
            k.op('sy', lambda h, d=dc, t_=th: h.dma_start(
                out=t['outx'][d * 128:(d + 1) * 128, t_ * TH:(t_ + 1) * TH],
                in_=b['xh'][:, d * TH:(d + 1) * TH]),
                inc=SO2, inc_n=16)
        k.waitop('sy', [(SO2, SO2.v)])
    k.waitop('gp', [(SO2, SO2.v)])
    with nc.Block() as blk:
        k.finalize(blk)
    return k


def build_gram(nc, t):
    k = K(nc)
    alloc_small(k)
    al = nc.alloc_sbuf_tensor
    b = k.bufs
    b['xh'] = al('xh', [128, 16 * TH], F32)[:]
    b['ytmp'] = al('ytmp', [128, 16 * TH], BF16)[:]
    b['y_tm'] = al('y_tm', [128, 16 * 2048], BF16)[:]
    b['yfc'] = al('yfc', [128, 128], F32)[:]
    b['egr'] = al('egr', [128, 1024], F32)[:]
    _common_small(k, t)
    k.gram_xh_guard = 0
    k.yfc_guard = 0
    emit_gram(k, t['xin'], t['epart'])
    SO = k.sem('SO')
    k.waitop('sy', [(SO, SO.v)])
    k.waitop('gp', [(SO, SO.v)])
    with nc.Block() as blk:
        k.finalize(blk)
    return k


def build_bpost(nc, t):
    k = K(nc)
    alloc_small(k)
    al = nc.alloc_sbuf_tensor
    b = k.bufs
    b['xh'] = al('xh', [128, 16 * TH], F32)[:]
    b['y_full'] = al('y_full', [128, 16 * 2048], BF16)[:]
    b['Et'] = al('Et', [128, 2048], F32)[:]
    b['Et2'] = al('Et2', [128, 2048], F32)[:]
    b['ebig'] = al('ebig', [128, 2048], BF16)[:]
    b['mpcols'] = al('mpcols', [128, 16], F32)[:]
    b['mprow'] = al('mprow', [1, 2048], F32)[:]
    b['scol'] = al('scol', [128, 1], F32)[:]
    b['scol2'] = al('scol2', [128, 1], F32)[:]
    b['etile'] = al('etile', [128, 256], F32)[:]
    b['asub'] = al('asub', [128, 256], F32)[:]
    b['abuf'] = al('abuf', [128, 256], BF16)[:]
    b['xrow'] = al('xrow', [128, 2048], F32)[:]
    _common_small(k, t)
    k.p1_guard = 0
    k.p3_guard = 0
    k.p3_mbc = 0
    k.sub_marks = [0, 0]
    k.exp_marks = [0, 0]
    k.mm_marks = [0, 0]
    emit_attnb_post(k, t['eps'], t['esum'], t['xin'], t['outx'])
    SO = k.sem('SO')
    k.waitop('sy', [(SO, SO.v)])
    k.waitop('gp', [(SO, SO.v)])
    with nc.Block() as blk:
        k.finalize(blk)
    return k



_CACHE = {}


def _host_prep(inputs):
    if 'prep' in _CACHE:
        return _CACHE['prep']
    gate_w = np.asarray(inputs['L0_ffn_gate_w'], np.float32)
    sgate_w = np.asarray(inputs['L0_ffn_shared_gate_w'], np.float32)
    sgu_w = np.asarray(inputs['L0_ffn_shared_gate_up_w'], np.float32)
    sdown_w = np.asarray(inputs['L0_ffn_shared_down_w'], np.float32)
    w13 = np.asarray(inputs['L0_ffn_w13'], np.float32)
    w2 = np.asarray(inputs['L0_ffn_w2'], np.float32)
    Wq = np.asarray(inputs['L0_attn_a_Wq'], np.float32)
    Wk = np.asarray(inputs['L0_attn_a_Wk'], np.float32)
    Wv = np.asarray(inputs['L0_attn_a_Wv'], np.float32)
    Wo = np.asarray(inputs['L0_attn_a_Wo'], np.float32)
    v0e = np.asarray(inputs['L0_attn_a_v_emb0'], np.float32)
    v1e = np.asarray(inputs['L0_attn_a_v_emb1'], np.float32)
    chunk_order = []
    for q in range(4):
        for ip in range(11):
            chunk_order += [11 * q + ip, 44 + 11 * q + ip]
    GUS = host_gu_slabs(sgu_w, chunk_order)
    RTS = np.concatenate([host_partial_slab(gate_w),
                          host_partial_slab(sgate_w)], 1)
    DWS = host_down_slabs(sdown_w, 44)
    e_order = []
    for ic in range(11):
        e_order += [ic, 11 + ic]
    E13 = np.concatenate([host_gu_slabs(w13[e], e_order) for e in range(8)], 0)
    E2 = np.concatenate([host_down_slabs(w2[e], 11) for e in range(8)], 0)
    SEL8 = np.zeros((8, 8 * 128), np.float32)
    for e in range(8):
        SEL8[e, e * 128:(e + 1) * 128] = 1.0
    IDN = np.eye(128, dtype=np.float32)
    WQS = host_partial_slab(Wq)
    WKS = host_partial_slab(Wk)
    WVS = host_partial_slab(Wv)
    WOS = np.zeros((128, 4 * 2048), np.float32)
    V0R = np.zeros((128, 4), np.float32)
    V1R = np.zeros((128, 4), np.float32)
    for g in range(4):
        for j in range(4):
            for v in range(8):
                WOS[32 * j + 1 + v, g * 2048:(g + 1) * 2048] = Wo[:, (4 * g + j) * 8 + v]
                V0R[32 * j + 1 + v, g] = v0e[(4 * g + j) * 8 + v]
                V1R[32 * j + 1 + v, g] = v1e[(4 * g + j) * 8 + v]
    import ml_dtypes
    KVEC = np.concatenate([np.full(8192, -3e38, np.float32),
                           np.arange(8192, dtype=np.float32)]).reshape(1, 16384)
    KVEC = KVEC.astype(ml_dtypes.bfloat16)
    TAU = np.asarray(inputs['L0_attn_a_tau'], np.float32).reshape(1, 1)
    prep = dict(GUS=GUS, RTS=RTS, DWS=DWS, E13=E13, E2=E2, SEL8=SEL8, IDN=IDN,
                WQS=WQS, WKS=WKS, WVS=WVS, WOS=WOS, V0R=V0R, V1R=V1R,
                KVEC=KVEC, TAU=TAU)
    _CACHE['prep'] = prep
    return prep


def _core_info(c):
    c2, bb = c % 4, c // 4
    Js = sorted([c2, 7 - c2, 8 + c2, 15 - c2])
    kA = [4 * J + 4 for J in Js]
    ci = np.zeros(256, np.int32)
    for blk in range(4):
        ci[blk] = Js[blk] * 512
        ci[4 + blk] = PADZ + (kA[blk] - PADK[blk]) * 128
        ci[8 + blk] = (64 + kA[blk] - PADK[blk]) * 36
        ci[12 + blk] = 8192 + (kA[blk] - PADK[blk]) * 128
    qp1 = np.zeros((1, 2048), np.float32)
    for blk in range(4):
        qp1[0, blk * 512:(blk + 1) * 512] = Js[blk] * 512 + np.arange(512) + 1
    rows = np.concatenate([np.arange(J * 512, (J + 1) * 512) for J in Js])
    return bb, rows, ci.reshape(1, 256), qp1


def _mk_attn_prog(prep):
    nc = bass.Bass("TRN2", target_bir_lowering=False, debug=False)
    t = {}
    t['xtf'] = nc.dram_tensor('xtf', [2048, 8192], F32, kind="ExternalInput").ap()
    t['xin'] = nc.dram_tensor('xin', [2048, 2048], F32, kind="ExternalInput").ap()
    t['outx'] = nc.dram_tensor('outx', [2048, 2048], F32, kind="ExternalOutput").ap()
    t['ci'] = nc.dram_tensor('ci', [1, 256], I32, kind="ExternalInput").ap()
    t['qp1'] = nc.dram_tensor('qp1', [1, 2048], F32, kind="ExternalInput").ap()
    t['kvec'] = nc.dram_tensor('kvec', [1, 16384], BF16, kind="ExternalInput").ap()
    for n, sh in [('wq_h', (128, 2048)), ('wk_h', (128, 512)), ('wv_h', (128, 512)),
                  ('wo_h', (128, 8192)), ('v0r', (128, 4)), ('v1r', (128, 4)),
                  ('tauh', (1, 1)), ('sel8c', (8, 1024)), ('idc', (128, 128))]:
        t[n] = nc.dram_tensor(n, list(sh), F32, kind="ExternalInput").ap()
    t['win_q'] = nc.dram_tensor('win_q', [4096, WQ], BF16, kind="Internal").ap()
    t['win_k'] = nc.dram_tensor('win_k', [1024, WK], BF16, kind="Internal").ap()
    t['vs_stage'] = nc.dram_tensor('vs_stage', [128, 128 * 36], BF16, kind="Internal").ap()
    t['loc_q'] = nc.dram_tensor('loc_q', [4096, 4 * 512], BF16, kind="Internal").ap()
    t['loc_k'] = nc.dram_tensor('loc_k', [1024, 160 * 128], BF16, kind="Internal").ap()
    t['loc_vs'] = nc.dram_tensor('loc_vs', [128, 160 * 36], BF16, kind="Internal").ap()
    t['loc_kv'] = nc.dram_tensor('loc_kv', [1, 160 * 128], BF16, kind="Internal").ap()
    build_attn(nc, t)
    return nc


def _mk_ffn_prog(prep):
    nc = bass.Bass("TRN2", target_bir_lowering=False, debug=False)
    t = {}
    t['xin'] = nc.dram_tensor('xin', [2048, 2048], F32, kind="ExternalInput").ap()
    t['outx'] = nc.dram_tensor('outx', [2048, 2048], F32, kind="ExternalOutput").ap()
    for n, a in [('gus', prep['GUS']), ('rts', prep['RTS']), ('dws', prep['DWS']),
                 ('e13', prep['E13']), ('e2', prep['E2']),
                 ('sel8c', prep['SEL8']), ('idc', prep['IDN'])]:
        t[n] = nc.dram_tensor(n, list(a.shape), F32, kind="ExternalInput").ap()
    build_ffn(nc, t)
    return nc


def _mk_gram_prog(prep):
    nc = bass.Bass("TRN2", target_bir_lowering=False, debug=False)
    t = {}
    t['xin'] = nc.dram_tensor('xin', [2048, 2048], F32, kind="ExternalInput").ap()
    t['epart'] = nc.dram_tensor('epart', [2048, 2048], F32, kind="ExternalOutput").ap()
    t['sel8c'] = nc.dram_tensor('sel8c', [8, 1024], F32, kind="ExternalInput").ap()
    t['idc'] = nc.dram_tensor('idc', [128, 128], F32, kind="ExternalInput").ap()
    build_gram(nc, t)
    return nc


def _mk_bpost_prog(prep):
    nc = bass.Bass("TRN2", target_bir_lowering=False, debug=False)
    t = {}
    t['xin'] = nc.dram_tensor('xin', [2048, 2048], F32, kind="ExternalInput").ap()
    for i in range(4):
        t[f'ep{i}'] = nc.dram_tensor(f'ep{i}', [2048, 2048], F32, kind="ExternalInput").ap()
    t['eps'] = [t[f'ep{i}'] for i in range(4)]
    t['esum'] = nc.dram_tensor('esum', [2048, 2048], F32, kind="Internal").ap()
    t['outx'] = nc.dram_tensor('outx', [2048, 2048], F32, kind="ExternalOutput").ap()
    t['sel8c'] = nc.dram_tensor('sel8c', [8, 1024], F32, kind="ExternalInput").ap()
    t['idc'] = nc.dram_tensor('idc', [128, 128], F32, kind="ExternalInput").ap()
    build_bpost(nc, t)
    return nc


def kernel(**inputs):
    prep = _host_prep(inputs)
    x0 = np.asarray(inputs['L0_attn_a_hidden_states'], np.float32)
    XT = [np.ascontiguousarray(x0[b].T) for b in range(2)]
    infos = [_core_info(c) for c in range(8)]

    wfeed = {'gus': prep['GUS'], 'rts': prep['RTS'], 'dws': prep['DWS'],
             'e13': prep['E13'], 'e2': prep['E2'],
             'sel8c': prep['SEL8'], 'idc': prep['IDN']}

    nc_a = _mk_attn_prog(prep)
    feeds = []
    for c in range(8):
        bb, rows, ci, qp1 = infos[c]
        feeds.append({'xtf': XT[bb], 'xin': np.ascontiguousarray(x0[bb][rows].T),
                      'ci': ci, 'qp1': qp1, 'kvec': prep['KVEC'],
                      'wq_h': prep['WQS'], 'wk_h': prep['WKS'],
                      'wv_h': prep['WVS'], 'wo_h': prep['WOS'],
                      'v0r': prep['V0R'], 'v1r': prep['V1R'],
                      'tauh': prep['TAU'], 'sel8c': prep['SEL8'],
                      'idc': prep['IDN']})
    r = run_bass_kernel_spmd(nc_a, feeds, core_ids=list(range(8)))
    x1sh = [np.ascontiguousarray(r.results[c]['outx']) for c in range(8)]

    nc_f = _mk_ffn_prog(prep)
    feeds = [dict(wfeed, xin=x1sh[c]) for c in range(8)]
    r = run_bass_kernel_spmd(nc_f, feeds, core_ids=list(range(8)))
    x2sh = [np.ascontiguousarray(r.results[c]['outx']) for c in range(8)]

    nc_g = _mk_gram_prog(prep)
    feeds = [{'xin': x2sh[c], 'sel8c': prep['SEL8'], 'idc': prep['IDN']}
             for c in range(8)]
    r = run_bass_kernel_spmd(nc_g, feeds, core_ids=list(range(8)))
    eparts = [np.ascontiguousarray(r.results[c]['epart']) for c in range(8)]

    nc_b = _mk_bpost_prog(prep)
    feeds = []
    for c in range(8):
        bb = c // 4
        fd = {'xin': x2sh[c], 'sel8c': prep['SEL8'], 'idc': prep['IDN']}
        for i, g in enumerate([4 * bb + j for j in range(4)]):
            fd[f'ep{i}'] = eparts[g]
        feeds.append(fd)
    r = run_bass_kernel_spmd(nc_b, feeds, core_ids=list(range(8)))
    x3sh = [np.ascontiguousarray(r.results[c]['outx']) for c in range(8)]

    feeds = [dict(wfeed, xin=x3sh[c]) for c in range(8)]
    r = run_bass_kernel_spmd(nc_f, feeds, core_ids=list(range(8)))

    out = np.zeros((2, 8192, 2048), np.float32)
    for c in range(8):
        bb, rows = infos[c][0], infos[c][1]
        out[bb][rows] = np.ascontiguousarray(r.results[c]['outx']).T
    return out


# version 06 (final)
# speedup vs baseline: 1.2415x; 1.2415x over previous
import numpy as np
import concourse.bass as bass
import concourse.mybir as mybir
from concourse.ap import AP
from concourse.bass_utils import run_bass_kernel_spmd

F32 = mybir.dt.float32
BF16 = mybir.dt.bfloat16
I32 = mybir.dt.int32
AF = mybir.ActivationFunctionType
ALU = mybir.AluOpType
AX = mybir.AxisListType.X

D = 2048
DC = 16
T = 2048
TH = 1024
TB = 512
EPS = 1e-5
ENGS = ('pe', 'act', 'dve', 'gp', 'sy')
PADK = [16, 32, 48, 64]
WQ = 8192
PADZ = 8192
WK = PADZ + 8192
SS = 8224
KZ = 1600
KVW = KZ + SS
QSP = 4 * 544
KSW = [PADK[b] * 128 + 31 for b in range(4)]
KS_COL = [0, 2079, 6206, 12381]
KSP = 20604

class Sem:
    def __init__(self, nc, name):
        self.h = nc.alloc_semaphore(name)
        self.v = 0
        self.name = name


class K:
    def __init__(self, nc):
        self.nc = nc
        self.streams = {e: [] for e in ENGS}
        self.sems = {}
        self.bufs = {}
        self.psum = {}
        self.wslab_marks = [0, 0]
        self.dslab_marks = [0, 0]

    def sem(self, name):
        if name not in self.sems:
            self.sems[name] = Sem(self.nc, name)
        return self.sems[name]

    def op(self, eng, fn, wait=(), inc=None, inc_n=1):
        if inc is not None:
            inc.v += inc_n
        wait = [(s, v) for (s, v) in wait if v > 0]

        def run(h, fn=fn, wait=tuple(wait), inc=inc, inc_n=inc_n):
            for s, v in wait:
                h.wait_ge(s.h, v)
            ins = fn(h)
            if inc is not None:
                ins.then_inc(inc.h, inc_n)
        self.streams[eng].append(run)

    def drain(self, eng):
        self.streams[eng].append(lambda h: h.drain())

    def waitop(self, eng, waits):
        for s, v in waits:
            if v > 0:
                self.streams[eng].append(
                    lambda h, s=s, v=v: h.wait_ge(s.h, v))

    def finalize(self, blk):
        st = self.streams
        blk.tensor(lambda h: [f(h) for f in st['pe']])
        blk.scalar(lambda h: [f(h) for f in st['act']])
        blk.vector(lambda h: [f(h) for f in st['dve']])
        blk.gpsimd(lambda h: [f(h) for f in st['gp']])
        blk.sync(lambda h: [f(h) for f in st['sy']])


    CONST_VALUES = (1e-5, -0.5, -1.0, 0.5, 17.0, -17.0, 1e-30, 1.0 / 16.0)

    def emit_const_init(self, sel8_hbm, id_hbm):
        nc = self.nc
        b = self.bufs
        V, S = self.sem('V'), self.sem('S')
        self.op('sy', lambda h: h.dma_start(out=b['sel8'][:], in_=sel8_hbm), inc=S, inc_n=16)
        self.op('sy', lambda h: h.dma_start(out=b['id_f32'][:], in_=id_hbm), inc=S, inc_n=16)
        self.op('dve', lambda h: h.memset(b['ones_col_bf'][:], 1.0), inc=V)
        self.op('dve', lambda h: h.memset(b['ones_row_f32'][:], 1.0), inc=V)
        cv = nc.alloc_sbuf_tensor('cvals', [1, max(16, len(self.CONST_VALUES))], F32)[:]
        for i, val in enumerate(self.CONST_VALUES):
            ap = cv[0:1, i:i + 1]
            self.op('dve', lambda h, a=ap, v=val: h.memset(a, v), inc=V)
            nc.const_aps.register(F32, val, ap)
        self.init_done = (self.sem('S').v, self.sem('V').v)


    def emit_router(self, yT, rt_hbm, th):
        b = self.bufs
        P, A, V, W = self.sem('P'), self.sem('A'), self.sem('V'), self.sem('W')
        b4, b5, b6, b7 = self.psum['b4'], self.psum['b5'], self.psum['b6'], self.psum['b7']
        b2, b3 = self.psum['b2'], self.psum['b3']
        wsl = b['wslab'][0]
        self.op('gp', lambda h: h.dma_start(out=wsl[:, 0:144], in_=rt_hbm),
                wait=[(P, max(self.wslab_marks[0], getattr(self, 'gp_gate', 0)))],
                inc=W, inc_n=16)
        wv = W.v
        for dc in range(16):
            for tb in range(2):
                self.op('pe', lambda h, d=dc, t_=tb:
                        h.matmul((b4 if t_ == 0 else b5)[0:8, :],
                                 wsl[:, d * 8:(d + 1) * 8],
                                 yT[:, d * TH + t_ * TB: d * TH + (t_ + 1) * TB],
                                 start=(d == 0), stop=(d == 15)),
                        wait=[(W, wv)] if (dc == 0 and tb == 0) else ())
            for tb in range(2):
                self.op('pe', lambda h, d=dc, t_=tb:
                        h.matmul((b6 if t_ == 0 else b7)[0:1, :],
                                 wsl[:, 128 + d:129 + d],
                                 yT[:, d * TH + t_ * TB: d * TH + (t_ + 1) * TB],
                                 start=(d == 0), stop=(d == 15)),
                        inc=(P if (dc == 15 and tb == 1) else None))
        pv = P.v
        self.wslab_marks[0] = pv
        self.op('dve', lambda h: h.tensor_copy(b['rl'][0:8, 0:TB], b4[0:8, :]),
                wait=[(P, pv)], inc=V)
        self.op('dve', lambda h: h.tensor_copy(b['rl'][0:8, TB:TH], b5[0:8, :]), inc=V)
        vv = V.v
        srow = b['rows'][:, 0:2 * TB]
        self.op('act', lambda h: h.activation(srow[:, 0:TB], b6[0:1, :], AF.Sigmoid),
                wait=[(P, pv)], inc=A)
        self.op('act', lambda h: h.activation(srow[:, TB:2 * TB], b7[0:1, :], AF.Sigmoid),
                inc=A)
        av = A.v
        self.op('pe', lambda h: h.matmul(b2[:, :], b['ones_row_f32'][0:1, :],
                                         srow[:, 0:TB], start=True, stop=True),
                wait=[(A, av)])
        self.op('pe', lambda h: h.matmul(b3[:, :], b['ones_row_f32'][0:1, :],
                                         srow[:, TB:2 * TB], start=True, stop=True), inc=P)
        pv2 = P.v
        self.op('dve', lambda h: h.tensor_copy(b['sig_bc'][:, 0:TB], b2[:, :]),
                wait=[(P, pv2)], inc=V)
        self.op('dve', lambda h: h.tensor_copy(b['sig_bc'][:, TB:TH], b3[:, :]), inc=V)
        self.drain('dve')
        lt, lte, ltp, ltm, ltw = b['lt'], b['lte'], b['ltp'], b['ltm'], b['ltw']
        c = b['cols']
        for st_ in range(8):
            self.op('pe', lambda h, s=st_: h.transpose(
                b6[:, 0:8], b['rl'][0:8, s * 128:(s + 1) * 128], b['id_f32'][0:8, 0:8]),
                wait=[(V, V.v)], inc=P)
            pv3 = P.v
            self.op('dve', lambda h: h.tensor_copy(lt[:], b6[:, 0:8]),
                    wait=[(P, pv3)], inc=V)
            self.drain('dve')
            self.op('dve', lambda h: h.reduce_max(c[:, 0:1], lt[:], axis=AX, negate=True),
                    inc=V)
            self.op('act', lambda h: h.activation(lte[:], lt[:], AF.Exp, bias=c[:, 0:1]),
                    wait=[(V, V.v)], inc=A)
            self.op('dve', lambda h: h.reduce_sum(c[:, 1:2], lte[:], axis=AX),
                    wait=[(A, A.v)], inc=V)
            self.drain('dve')
            self.op('dve', lambda h: h.reciprocal(c[:, 2:3], c[:, 1:2]), inc=V)
            self.drain('dve')
            self.op('dve', lambda h: h.tensor_scalar_mul(ltp[:], lte[:], c[:, 2:3]), inc=V)
            self.drain('dve')
            self.op('dve', lambda h: h.reduce_max(c[:, 3:4], ltp[:], axis=AX), inc=V)
            self.drain('dve')
            self.op('dve', lambda h: h.tensor_scalar(ltm[:], ltp[:], c[:, 3:4], None,
                                                     ALU.is_ge), inc=V)
            self.drain('dve')
            self.op('dve', lambda h: h.scalar_tensor_tensor(lte[:], ltm[:], -1e30, ltp[:],
                                                            ALU.mult, ALU.add), inc=V)
            self.drain('dve')
            self.op('dve', lambda h: h.reduce_max(c[:, 4:5], lte[:], axis=AX), inc=V)
            self.drain('dve')
            self.op('dve', lambda h: h.tensor_scalar(ltm[:], ltp[:], c[:, 4:5], None,
                                                     ALU.is_ge), inc=V)
            self.drain('dve')
            self.op('dve', lambda h: h.tensor_mul(ltw[:], ltp[:], ltm[:]), inc=V)
            self.drain('dve')
            self.op('pe', lambda h: h.transpose(b7[0:8, 0:128], ltw[:], b['id_f32'][:, :]),
                    wait=[(V, V.v)], inc=P)
            pv4 = P.v
            self.op('dve', lambda h, s=st_: h.tensor_copy(
                b['w_rows'][0:8, s * 128:(s + 1) * 128], b7[0:8, 0:128]),
                wait=[(P, pv4)], inc=V)
        self.waitop('pe', [(V, V.v)])
        self.rt_done = V.v


    def emit_we_bcast(self, e):
        b = self.bufs
        P, V = self.sem('P'), self.sem('V')
        b2, b3 = self.psum['b2'], self.psum['b3']
        self.op('pe', lambda h, e=e: h.matmul(b2[:, :], b['sel8'][:, e * 128:(e + 1) * 128],
                                         b['w_rows'][0:8, 0:TB], start=True, stop=True))
        self.op('pe', lambda h, e=e: h.matmul(b3[:, :], b['sel8'][:, e * 128:(e + 1) * 128],
                                         b['w_rows'][0:8, TB:2 * TB], start=True, stop=True),
                inc=P)
        pv = P.v
        self.op('dve', lambda h: h.tensor_copy(b['we_bc'][:, 0:TB], b2[:, :]),
                wait=[(P, pv)], inc=V)
        self.op('dve', lambda h: h.tensor_copy(b['we_bc'][:, TB:TH], b3[:, :]), inc=V)
        self.drain('dve')
        self.waitop('pe', [(V, V.v)])


    def emit_quarter(self, xh, yT, gu_slabs, down_slabs, n_pairs, fold_sb, h_guard):
        b = self.bufs
        P, A, V, W = self.sem('P'), self.sem('A'), self.sem('V'), self.sem('W')
        wsl, dsl = b['wslab'], b['dslab']
        gu_hbm = gu_slabs
        GB = [[self.psum['b0'], self.psum['b1']], [self.psum['b4'], self.psum['b5']]]
        UB = [[self.psum['b2'], self.psum['b3']], [self.psum['b6'], self.psum['b7']]]
        slab_consumed = []
        sil_marks = []
        hm_marks = []
        for ip in range(n_pairs):
            par = ip % 2
            for role in range(2):
                sl = 2 * ip + role
                bi = sl % 2
                gw = [(P, slab_consumed[sl - 2])] if sl >= 2 else \
                     [(P, max(self.wslab_marks[bi], getattr(self, 'gp_gate', 0)))]
                self.op('gp', lambda h, s=sl, bi=bi:
                        h.dma_start(out=wsl[bi][:], in_=gu_hbm[s]),
                        wait=gw, inc=W, inc_n=16)
                wv = W.v
                pw = [(W, wv)]
                if role == 0 and ip >= 2:
                    pw = pw + [(A, sil_marks[2 * (ip - 2) + 1]),
                               (V, hm_marks[2 * (ip - 2) + 1])]
                for dc in range(16):
                    for tb in range(2):
                        bank = (GB if role == 0 else UB)[par][tb]
                        last = (dc == 15 and tb == 1)
                        self.op('pe', lambda h, bi=bi, d=dc, t_=tb, bk=bank,
                                st=(dc == 0), sp=(dc == 15):
                                h.matmul(bk[:, :], wsl[bi][:, d * 128:(d + 1) * 128],
                                         yT[:, d * TH + t_ * TB: d * TH + (t_ + 1) * TB],
                                         start=st, stop=sp),
                                wait=pw if (dc == 0 and tb == 0) else (),
                                inc=(P if last else None))
                slab_consumed.append(P.v)
                self.wslab_marks[bi] = P.v
            wc_last = slab_consumed[-1]
            for tb in range(2):
                tsl = b['tmp_silu'][:, (2 * par + tb) * TB:(2 * par + tb + 1) * TB]
                awq = [(P, wc_last)]
                if ip >= 2:
                    awq.append((V, hm_marks[2 * (ip - 2) + tb]))
                self.op('act', lambda h, o=tsl, g=GB[par][tb]:
                        h.activation(o, g[:, :], AF.Silu),
                        wait=awq, inc=A)
                sil_marks.append(A.v)
            for tb in range(2):
                tsl = b['tmp_silu'][:, (2 * par + tb) * TB:(2 * par + tb + 1) * TB]
                hsl = b['h'][:, ip * TH + tb * TB: ip * TH + (tb + 1) * TB]
                dw = [(A, sil_marks[2 * ip + tb])]
                if ip == 0 and tb == 0 and h_guard > 0:
                    dw = dw + [(P, h_guard)]
                self.op('dve', lambda h, o=hsl, a=tsl, u=UB[par][tb]:
                        h.tensor_mul(o, a, u[:, :]), wait=dw,
                        inc=(V if fold_sb is None else None))
                if fold_sb is None:
                    hm_marks.append(V.v)
            if fold_sb is not None:
                self.drain('dve')
                for tb in range(2):
                    hsl = b['h'][:, ip * TH + tb * TB: ip * TH + (tb + 1) * TB]
                    self.op('dve', lambda h, o=hsl, f=fold_sb, t_=tb:
                            h.tensor_mul(o, o, f[:, t_ * TB:(t_ + 1) * TB]), inc=V)
                    hm_marks.append(V.v)
        DBK = [[self.psum['b0'], self.psum['b1']], [self.psum['b2'], self.psum['b3']]]
        dsl_consumed = []
        add_marks = []
        for dc in range(16):
            par = dc % 2
            gw = [(P, dsl_consumed[dc - 2])] if dc >= 2 else \
                 [(P, max(self.dslab_marks[par], slab_consumed[-1]))]
            self.op('gp', lambda h, d=dc, bi=par, n=n_pairs:
                    h.dma_start(out=dsl[bi][:, 0:n * 128], in_=down_slabs[d]),
                    wait=gw, inc=W, inc_n=16)
            wv = W.v
            pw = [(W, wv)]
            if dc == 0:
                pw = pw + [(V, hm_marks[-1]), (A, sil_marks[-1])]
            if dc >= 2:
                pw = pw + [(V, add_marks[dc - 2])]
            for ic in range(n_pairs):
                for tb in range(2):
                    last = (ic == n_pairs - 1 and tb == 1)
                    self.op('pe', lambda h, bi=par, i=ic, t_=tb, bk=DBK[par][tb],
                            st=(ic == 0), sp=(ic == n_pairs - 1):
                            h.matmul(bk[:, :], dsl[bi][:, i * 128:(i + 1) * 128],
                                     b['h'][:, i * TH + t_ * TB: i * TH + (t_ + 1) * TB],
                                     start=st, stop=sp),
                            wait=pw if (ic == 0 and tb == 0) else (),
                            inc=(P if last else None))
            dsl_consumed.append(P.v)
            self.dslab_marks[par] = P.v
            for tb in range(2):
                xsl = xh[:, dc * TH + tb * TB: dc * TH + (tb + 1) * TB]
                self.op('dve', lambda h, x=xsl, bk=DBK[par][tb]:
                        h.tensor_add(x, x, bk[:, :]),
                        wait=[(P, dsl_consumed[-1])] if tb == 0 else (), inc=V)
            add_marks.append(V.v)
        self.waitop('pe', [(V, add_marks[-1]), (V, add_marks[-2])])
        self.gp_gate = dsl_consumed[-1]
        return dsl_consumed[-1]



def host_gu_slabs(gu, chunk_order):
    out = []
    for ic in chunk_order:
        A = gu[ic * 128:(ic + 1) * 128, :]
        out.append(np.ascontiguousarray(
            A.reshape(128, 16, 128).transpose(2, 1, 0).reshape(128, 2048)))
    return np.ascontiguousarray(np.stack(out))


def host_partial_slab(tail):
    R = tail.shape[0]
    return np.ascontiguousarray(
        tail.reshape(R, 16, 128).transpose(2, 1, 0).reshape(128, 16 * R))


def host_down_slabs(wd, n_ic):
    I = wd.shape[1]
    assert I == n_ic * 128
    out = []
    for dc in range(16):
        B = wd[dc * 128:(dc + 1) * 128, :]
        out.append(np.ascontiguousarray(
            B.reshape(128, n_ic, 128).transpose(2, 1, 0).reshape(128, I)))
    return np.ascontiguousarray(np.stack(out))


def emit_attn_consts(k, tau_hbm):
    b = k.bufs
    P, V, S = k.sem('P'), k.sem('V'), k.sem('S')
    tt = b['tinyf']
    k.op('sy', lambda h: h.dma_start(out=tt[0:1, 0:1], in_=tau_hbm), inc=S, inc_n=16)
    sv = S.v
    k.op('dve', lambda h: h.reciprocal(tt[0:1, 1:2], tt[0:1, 0:1]),
         wait=[(S, sv)], inc=V)
    k.drain('dve')
    vv = V.v
    k.op('pe', lambda h: h.matmul(k.psum['b0'][:, 0:1], b['ones_row_f32'][0:1, :],
                                  tt[0:1, 1:2], start=True, stop=True),
         wait=[(V, vv), (V, k.init_done[1])], inc=P)
    pv = P.v
    ac = b['acols']
    k.op('dve', lambda h: h.tensor_copy(ac[:, 0:1], k.psum['b0'][:, 0:1]),
         wait=[(P, pv)], inc=V)
    k.drain('dve')
    k.op('dve', lambda h: h.tensor_scalar_mul(ac[:, 1:2], ac[:, 0:1], 0.5), inc=V)
    k.op('dve', lambda h: h.tensor_scalar_mul(ac[:, 2:3], ac[:, 0:1], -17.0), inc=V)
    k.drain('dve')
    k.op('dve', lambda h: h.tensor_mul(ac[:, 3:4], ac[:, 0:1], ac[:, 0:1]), inc=V)
    k.drain('dve')
    k.op('dve', lambda h: h.tensor_scalar_mul(ac[:, 3:4], ac[:, 3:4], 1.0 / 16.0), inc=V)
    k.op('dve', lambda h: h.tensor_sub(b['demb'][:], b['vemb1r'][:], b['vemb0r'][:]),
         inc=V)
    k.drain('dve')
    k.waitop('pe', [(k.sem('V'), k.sem('V').v)])
    k.attn_const_done = k.sem('V').v


def emit_attn_prologue(k, xTfull, qdram, kvdram, kvec_hbm):
    b = k.bufs
    nc = k.nc
    P, A, V, S, W = k.sem('P'), k.sem('A'), k.sem('V'), k.sem('S'), k.sem('W')
    ac = b['acols']
    r_col, rh_col = ac[:, 0:1], ac[:, 1:2]
    s_col = ac[:, 3:4]
    bq, bk, bv = k.psum['b4'], k.psum['b5'], k.psum['b6']
    k.op('dve', lambda h: h.memset(b['q_raw'][:, 0:31], 0.0), inc=V)
    k.op('dve', lambda h: h.memset(b['q_raw'][:, SS - 1:SS], 0.0), inc=V)
    k.op('dve', lambda h: h.memset(b['kv_raw'][:, 0:31], 0.0), inc=V)
    k.op('dve', lambda h: h.memset(b['kv_raw'][:, SS - 1:SS], 0.0), inc=V)
    k.op('dve', lambda h: h.memset(b['vs_tm'][:], 0.0), inc=V)
    zv = V.v
    prev_evac = 0
    prev_pe = 0
    for tt in range(16):
        s0 = S.v
        for dc in range(16):
            k.op('sy', lambda h, d=dc, t=tt: h.dma_start(
                out=b['xblk'][:, d * TB:(d + 1) * TB],
                in_=xTfull[d * 128:(d + 1) * 128, t * TB:(t + 1) * TB]),
                wait=[(V, prev_evac)] if dc == 0 else (), inc=S, inc_n=16)
        k.waitop('dve', [(S, S.v)])
        emit_ln_gen(k, b['xblk'], b['ytile'], 1, TB, TB)
        vln = V.v
        for dc in range(16):
            w = [(V, vln), (V, k.attn_const_done)] if dc == 0 else ()
            k.op('pe', lambda h, d=dc, st=(dc == 0), sp=(dc == 15):
                 h.matmul(bq[:, :], b['wq_s'][:, d * 128:(d + 1) * 128],
                          b['ytile'][:, d * TB:(d + 1) * TB], start=st, stop=sp),
                 wait=w)
            k.op('pe', lambda h, d=dc, st=(dc == 0), sp=(dc == 15):
                 h.matmul(bk[0:32, :], b['wk_s'][:, d * 32:(d + 1) * 32],
                          b['ytile'][:, d * TB:(d + 1) * TB], start=st, stop=sp))
            k.op('pe', lambda h, d=dc, st=(dc == 0), sp=(dc == 15):
                 h.matmul(bv[0:32, :], b['wv_s'][:, d * 32:(d + 1) * 32],
                          b['ytile'][:, d * TB:(d + 1) * TB], start=st, stop=sp),
                 inc=(P if dc == 15 else None))
        pv = P.v
        col = 31 + tt * TB
        k.op('act', lambda h, c=col: h.activation(
            b['q_raw'][:, c:c + TB], bq[:, :], AF.Tanh, scale=r_col),
            wait=[(P, pv), (V, zv)], inc=A)
        k.op('act', lambda h, c=col: h.activation(
            b['kv_raw'][0:32, c:c + TB], bk[0:32, :], AF.Tanh, scale=r_col[0:32, :]),
            inc=A)
        k.op('act', lambda h, c=col: h.activation(
            b['ebuf'][0:32, 0:TB], bv[0:32, :], AF.Tanh, scale=rh_col[0:32, :]),
            inc=A)
        av = A.v
        k.op('dve', lambda h, c=col: h.tensor_scalar(
            b['kv_raw'][32:64, c:c + TB], b['ebuf'][0:32, 0:TB], 0.5, 0.5,
            ALU.mult, ALU.add), wait=[(A, av)], inc=V)
        k.drain('dve')
        k.op('dve', lambda h, c=col: h.tensor_scalar_mul(
            b['kv_raw'][0:32, c:c + TB], b['kv_raw'][0:32, c:c + TB],
            s_col[0:32, :]), inc=V)
        prev_evac = V.v
        prev_pe = P.v
    k.waitop('sy', [(V, prev_evac), (A, k.sem('A').v)])
    k.op('sy', lambda h: h.dma_start(out=qdram, in_=b['q_raw']), inc=S, inc_n=16)
    k.op('sy', lambda h: h.dma_start(out=kvdram[:, KZ:KZ + SS], in_=b['kv_raw']),
         inc=S, inc_n=16)
    k.op('sy', lambda h: h.dma_start(out=kvdram[:, 0:KZ],
                                     in_=b['vs_tm'][0:64, 0:KZ]),
         wait=[(V, zv)], inc=S, inc_n=16)
    k.win_done = S.v
    vc = b['vcop']
    for kt in range(64):
        k.op('dve', lambda h, kt=kt: h.tensor_copy(
            vc[:], b['kv_raw'][32:64, 32 + kt * 128: 32 + (kt + 1) * 128]),
            wait=[(V, prev_evac)] if kt == 0 else (), inc=V)
        vv = V.v
        k.op('pe', lambda h: h.transpose(k.psum['b7'][:, 0:32], vc[:],
                                         b['id_f32'][0:32, 0:32]),
             wait=[(V, vv)], inc=P)
        pv = P.v
        for g in range(4):
            k.op('dve', lambda h, kt=kt, g=g: h.tensor_copy(
                b['vs_tm'][:, (64 + kt) * 36 + g * 9 + 1: (64 + kt) * 36 + g * 9 + 9],
                k.psum['b7'][:, g * 8:(g + 1) * 8]),
                wait=[(P, pv)] if g == 0 else (), inc=V)
        k.drain('dve')
    for kt in range(64):
        for g in range(4):
            k.op('dve', lambda h, kt=kt, g=g: h.memset(
                b['vs_tm'][:, (64 + kt) * 36 + g * 9: (64 + kt) * 36 + g * 9 + 1], 1.0),
                inc=V)
    vsv = k.sem('V').v
    k.op('sy', lambda h: h.dma_start(out=k.vs_stage, in_=b['vs_tm']),
         wait=[(V, vsv)], inc=S, inc_n=16)
    k.vs_done = S.v


def bass_zero_src(k):
    return k.bufs['vs_tm'][:, 0:2048]


def emit_attn_main(k, coreinfo_sb, kvec2_hbm, qp1_hbm):
    b = k.bufs
    nc = k.nc
    P, A, V, S, W, GA = (k.sem('P'), k.sem('A'), k.sem('V'), k.sem('S'),
                         k.sem('W'), k.sem('GA'))
    if not hasattr(k, 'gpregs'):
        k.gpregs = [nc.sync.alloc_register(f'symofs{i}') for i in range(4)]
    ac = b['acols']
    m17_col = ac[:, 2:3]
    sb = [k.psum['b0'], k.psum['b1'], k.psum['b2'], k.psum['b3']]
    pvb = k.psum['b4']
    epb = k.psum['b5']
    ebuf = b['ebuf']
    qp1sb = b['qp1sb']
    k.op('sy', lambda h: h.dma_start(out=qp1sb, in_=qp1_hbm), inc=S, inc_n=16)
    sv = S.v
    tt = b['tinyf']
    k.op('dve', lambda h: h.tensor_scalar_mul(qp1sb, qp1sb, tt[0:1, 0:1]),
         wait=[(S, sv), (V, k.attn_const_done)], inc=V)
    k.drain('dve')
    k.op('dve', lambda h: h.reciprocal(b['crow_f'], qp1sb), inc=V)
    k.drain('dve')
    k.op('dve', lambda h: h.tensor_copy(b['crow'], b['crow_f']), inc=V)
    k.drain('dve')
    crow_done = V.v

    LOCK_COL = [0, 16 * 128, 48 * 128, 96 * 128]
    VS_COL = [0, 16 * 36, 48 * 36, 96 * 36]
    KV_COL = [0, 16 * 128, 48 * 128, 96 * 128]
    rwait = [(S, max(k.win_done, k.vs_done))]
    expmark = [0] * 4
    for blk in range(4):
        padK = PADK[blk]
        def rq(h, blk=blk):
            h.reg_load(k.gpregs[0], coreinfo_sb[0:1, blk:blk + 1])
            return h.dma_start(
                out=AP(tensor=k.qsel.tensor, offset=blk * 544,
                       ap=[[QSP, 128], [1, 543]]),
                in_=AP(tensor=k.qdram.tensor, offset=k.gpregs[0],
                       ap=[[SS, 128], [1, 543]]))
        k.op('sy', rq, wait=rwait if blk == 0 else (), inc=S, inc_n=16)
        def rk(h, blk=blk, padK=padK):
            h.reg_load(k.gpregs[1], coreinfo_sb[0:1, 4 + blk:5 + blk])
            return h.dma_start(
                out=AP(tensor=k.kvsel.tensor, offset=KS_COL[blk],
                       ap=[[KSP, 64], [1, KSW[blk]]]),
                in_=AP(tensor=k.kvdram.tensor, offset=k.gpregs[1],
                       ap=[[KVW, 64], [1, KSW[blk]]]))
        k.op('sy', rk, inc=S, inc_n=16)
        def rv(h, blk=blk, padK=padK):
            h.reg_load(k.gpregs[2], coreinfo_sb[0:1, 8 + blk:9 + blk])
            return h.dma_start(
                out=k.loc_vs[:, VS_COL[blk]:VS_COL[blk] + padK * 36],
                in_=AP(tensor=k.vs_stage.tensor, offset=k.gpregs[2],
                       ap=[[128 * 36, 128], [1, padK * 36]]))
        k.op('sy', rv, inc=S, inc_n=16)
        def rkv(h, blk=blk, padK=padK):
            h.reg_load(k.gpregs[3], coreinfo_sb[0:1, 12 + blk:13 + blk])
            return h.dma_start(
                out=b['kvec2d'][:, blk * 64: blk * 64 + padK],
                in_=AP(tensor=kvec2_hbm.tensor, offset=k.gpregs[3],
                       ap=[[128, 128], [1, padK]]))
        k.op('sy', rkv, inc=S, inc_n=16)
        k.waitop('sy', [(S, S.v)])
        for hh in range(16):
            k.op('sy', lambda h, blk=blk, hh=hh: h.dma_start(
                out=AP(tensor=k.loc_q.tensor, offset=256 * hh * 2048 + blk * 512,
                       ap=[[8 * 2048, 32], [2048, 8], [1, 512]]),
                in_=AP(tensor=k.qsel.tensor, offset=8 * hh * QSP + blk * 544,
                       ap=[[1, 32], [QSP, 8], [1, 512]])), inc=S, inc_n=16)
        for g in range(4):
            k.op('sy', lambda h, blk=blk, g=g, padK=padK: h.dma_start(
                out=AP(tensor=k.loc_k.tensor, offset=256 * g * 20480 + LOCK_COL[blk],
                       ap=[[8 * 20480, 32], [20480, 8], [1, padK * 128]]),
                in_=AP(tensor=k.kvsel.tensor, offset=8 * g * KSP + KS_COL[blk],
                       ap=[[1, 32], [KSP, 8], [1, padK * 128]])), inc=S, inc_n=16)
        k.waitop('sy', [(S, S.v)])
        expmark[blk] = S.v

    a_pro = A.v
    p_pro = P.v
    k.op('dve', lambda h: h.memset(b['ones_row_bf'], 1.0), inc=V)
    orb_done = V.v
    for blk in range(4):
        padK = PADK[blk]
        cbw = [(V, crow_done), (V, orb_done), (V, getattr(k, 'cb_guard', 0))]
        k.op('pe', lambda h, blk=blk: h.matmul(
            k.psum['b6'][:, :], b['ones_row_bf'][0:1, :],
            b['crow'][0:1, blk * TB:(blk + 1) * TB],
            start=True, stop=True), wait=cbw, inc=P)
        cbp = P.v
        k.op('dve', lambda h: h.tensor_copy(b['crow_bc'], k.psum['b6'][:, :]),
             wait=[(P, cbp)], inc=V)
        k.cb_guard = V.v
        k.op('dve', lambda h, blk=blk, padK=padK: h.tensor_copy(
            b['kvec2f'][:, blk * 64: blk * 64 + padK],
            b['kvec2d'][:, blk * 64: blk * 64 + padK]),
            wait=[(S, expmark[blk])], inc=V)
        k.drain('dve')
        for g in range(4):
            gw = [(S, expmark[blk]), (P, getattr(k, 'kw_guard', 0))]
            for c in range(2):
                k.op('gp', lambda h, g=g, blk=blk, c=c, padK=padK: h.dma_start(
                    out=b['kwbuf'][:, c * 8192: c * 8192 + padK * 128],
                    in_=k.loc_k[g * 256 + 128 * c: g * 256 + 128 * (c + 1),
                                LOCK_COL[blk]:LOCK_COL[blk] + padK * 128]),
                    wait=gw if c == 0 else (), inc=W, inc_n=16)
            for j in range(4):
                hh = g * 4 + j
                for c in range(2):
                    k.op('gp', lambda h, hh=hh, c=c, blk=blk, j=j: h.dma_start(
                        out=b['qwbuf'][:, (j * 2 + c) * TB: (j * 2 + c + 1) * TB],
                        in_=k.loc_q[256 * hh + 128 * c: 256 * hh + 128 * (c + 1),
                                    blk * 512:(blk + 1) * 512]),
                        inc=W, inc_n=16)
            k.op('gp', lambda h, padK=padK, blk=blk: h.dma_start(
                out=b['vsblk'][:, 0:padK * 36],
                in_=k.loc_vs[:, VS_COL[blk]:VS_COL[blk] + padK * 36]),
                inc=W, inc_n=16)
            wv = W.v
            exp_marks = list(getattr(k, 'prev_exp_marks', [0] * 4))
            sel_marks = [0] * 4
            ts_marks = [0] * 4
            last_pv = 0
            prev = None
            for kt in range(padK + 1):
                if prev is not None:
                    pkt, pdiag = prev
                    for j in range(4):
                        pw = [(GA, sel_marks[j])] if pdiag >= 0 else [(A, exp_marks[j])]
                        if pkt == 0 and j == 0:
                            pw = pw + [(V, getattr(k, 'pv_guard', 0))]
                        k.op('pe', lambda h, j=j, kt=pkt, g=g, padK=padK: h.matmul(
                            pvb[32 * j: 32 * j + 9, :],
                            b['vsblk'][:, kt * 36 + g * 9: kt * 36 + (g + 1) * 9],
                            ebuf[:, j * TB:(j + 1) * TB],
                            start=(kt == 0), stop=(kt == padK - 1),
                            tile_position=(0, 32 * j),
                            skip_group_check=(j > 0)), wait=pw,
                            inc=(P if j == 3 else None))
                    last_pv = P.v
                if kt == padK:
                    break
                diag_t = kt - (padK - 4)
                for j in range(4):
                    tw = [(A, exp_marks[j])]
                    if blk == 0 and g == 0 and kt == 0:
                        tw = tw + [(P, p_pro)]
                    k.op('dve', lambda h, j=j, kt=kt, blk=blk: h.tensor_scalar_mul(
                        sb[j][:, :], b['crow_bc'],
                        b['kvec2f'][:, blk * 64 + kt: blk * 64 + kt + 1]),
                        wait=tw, inc=V)
                    ts_marks[j] = V.v
                    pw = [(V, ts_marks[j])]
                    if kt == 0 and j == 0:
                        pw = pw + [(W, wv)]
                    k.op('pe', lambda h, j=j, kt=kt: h.matmul(
                        sb[j][:, :], b['kwbuf'][:, kt * 128:(kt + 1) * 128],
                        b['qwbuf'][:, (j * 2) * TB:(j * 2 + 1) * TB],
                        start=False, stop=False, skip_group_check=True), wait=pw)
                    k.op('pe', lambda h, j=j, kt=kt: h.matmul(
                        sb[j][:, :], b['kwbuf'][:, 8192 + kt * 128: 8192 + (kt + 1) * 128],
                        b['qwbuf'][:, (j * 2 + 1) * TB:(j * 2 + 2) * TB],
                        start=False, stop=True, skip_group_check=True), inc=P)
                    pvmm = P.v
                    k.op('act', lambda h, j=j: h.activation(
                        ebuf[:, j * TB:(j + 1) * TB], sb[j][:, :], AF.Exp,
                        bias=m17_col),
                        wait=[(P, pvmm), (P, last_pv)], inc=A)
                    exp_marks[j] = A.v
                    if diag_t >= 0:
                        k.op('gp', lambda h, j=j, dt=diag_t: h.affine_select(
                            ebuf[:, j * TB:(j + 1) * TB],
                            ebuf[:, j * TB:(j + 1) * TB],
                            pattern=[[1, TB]], compare_op=ALU.is_gt, fill=0.0,
                            base=-128 * dt, channel_multiplier=-1),
                            wait=[(A, exp_marks[j])], inc=GA)
                        sel_marks[j] = k.sem('GA').v
                prev = (kt, diag_t)
            k.prev_exp_marks = list(exp_marks)
            k.kw_guard = P.v
            pvdone = P.v
            k.op('dve', lambda h: h.tensor_copy(b['pv_sb'][:], pvb[:, :]),
                 wait=[(P, pvdone)], inc=V)
            k.drain('dve')
            rows = b['rows']
            for j in range(4):
                rm = rows[:, j * TB:(j + 1) * TB]
                k.op('dve', lambda h, j=j, rm=rm: h.tensor_scalar_max(
                    rm, b['pv_sb'][32 * j: 32 * j + 1, :], 1e-30), inc=V)
            k.drain('dve')
            for j in range(4):
                rm = rows[:, j * TB:(j + 1) * TB]
                k.op('dve', lambda h, rm=rm: h.reciprocal(rm, rm), inc=V)
            k.drain('dve')
            vv = V.v
            for j in range(4):
                rm = rows[:, j * TB:(j + 1) * TB]
                k.op('pe', lambda h, j=j, rm=rm: h.matmul(
                    epb[32 * j: 32 * j + 32, :], b['ones_row_f32'][0:1, 0:32],
                    rm, start=True, stop=True, tile_position=(0, 32 * j),
                    skip_group_check=(j > 0)),
                    wait=[(V, vv)] if j == 0 else (), inc=(P if j == 3 else None))
            pv2 = P.v
            ocol = (blk * 4 + g) * TB
            for j in range(4):
                k.op('dve', lambda h, j=j, ocol=ocol: h.tensor_mul(
                    b['outat'][32 * j:32 * j + 32, ocol:ocol + TB],
                    b['pv_sb'][32 * j:32 * j + 32, :],
                    epb[32 * j:32 * j + 32, :]),
                    wait=[(P, pv2)] if j == 0 else (), inc=V)
            k.drain('dve')
            for j in range(4):
                k.op('dve', lambda h, j=j, ocol=ocol, g=g: h.tensor_scalar(
                    b['outat'][32 * j:32 * j + 32, ocol:ocol + TB],
                    b['outat'][32 * j:32 * j + 32, ocol:ocol + TB],
                    b['demb'][32 * j:32 * j + 32, g:g + 1],
                    b['vemb0r'][32 * j:32 * j + 32, g:g + 1],
                    ALU.mult, ALU.add), inc=V)
            k.drain('dve')
            k.pv_guard = V.v


def emit_attn_final(k, xin, outx):
    b = k.bufs
    P, V, S = k.sem('P'), k.sem('V'), k.sem('S')
    SO = k.sem('SO')
    att_done = V.v
    k.xadd_marks = [0] * 16
    for blk in range(4):
        s0 = S.v
        for dc in range(16):
            w0 = [(SO, k.sem('SO').v), (V, att_done)] if dc == 0 else ()
            k.op('sy', lambda h, d=dc, blk=blk: h.dma_start(
                out=b['xblk'][:, d * TB:(d + 1) * TB],
                in_=xin[d * 128:(d + 1) * 128, blk * TB:(blk + 1) * TB]),
                wait=w0, inc=S, inc_n=16)
        sx = S.v
        for dc in range(16):
            par = dc % 2
            bank = k.psum['b6'] if par == 0 else k.psum['b7']
            pw = [(V, att_done), (S, k.sem('S').v)] if dc == 0 else ()
            if dc >= 2:
                pw = [(V, k.xadd_marks[dc - 2])]
            for g in range(4):
                k.op('pe', lambda h, d=dc, g=g, blk=blk, bank=bank: h.matmul(
                    bank[:, :], b['wo_s'][:, g * 2048 + d * 128: g * 2048 + (d + 1) * 128],
                    b['outat'][:, (blk * 4 + g) * TB:(blk * 4 + g + 1) * TB],
                    start=(g == 0), stop=(g == 3)),
                    wait=pw if g == 0 else (), inc=(P if g == 3 else None))
            pv = P.v
            k.op('dve', lambda h, d=dc, bank=bank: h.tensor_add(
                b['xblk'][:, d * TB:(d + 1) * TB],
                b['xblk'][:, d * TB:(d + 1) * TB], bank[:, :]),
                wait=[(P, pv)], inc=V)
            k.xadd_marks[dc] = V.v
        k.waitop('sy', [(V, k.sem('V').v)])
        for dc in range(16):
            k.op('sy', lambda h, d=dc, blk=blk: h.dma_start(
                out=outx[d * 128:(d + 1) * 128, blk * TB:(blk + 1) * TB],
                in_=b['xblk'][:, d * TB:(d + 1) * TB]),
                inc=SO, inc_n=16)


def emit_ln_gen(k, xh, yT, n_tb, pitch_x, pitch_y, ytb0=0):
    b = k.bufs
    P, A, V = k.sem('P'), k.sem('A'), k.sem('V')
    bsum, bsq, ba, bb = k.psum['b0'], k.psum['b1'], k.psum['b2'], k.psum['b3']
    self = k
    self.waitop('dve', [(P, P.v)])
    for tb in range(n_tb):
        t0 = tb * TB
        pe_marks = []
        v0 = V.v
        a0 = A.v
        for dc in range(16):
            par = dc % 2
            xsl = xh[:, dc * pitch_x + t0: dc * pitch_x + t0 + TB]
            xb = b['ln_xbf'][:, par * TB:(par + 1) * TB]
            sq = b['ln_sq'][:, par * TB:(par + 1) * TB]
            w = [(P, pe_marks[dc - 2])] if dc >= 2 else []
            self.op('dve', lambda h, o=xb, i=xsl: h.tensor_copy(o, i), wait=w, inc=V)
            aw = [(V, v0 + dc + 1)]
            if dc >= 2:
                aw.append((P, pe_marks[dc - 2]))
            self.op('act', lambda h, o=sq, i=xb: h.activation(o, i, AF.Square),
                    wait=aw, inc=A)
            iw = [(V, v0 + dc + 1)]
            if dc == 0:
                iw.append((V, self.init_done[1]))
            self.op('pe', lambda h, i=xb, st=(dc == 0), sp=(dc == 15):
                    h.matmul(bsum[0:1, :], b['ones_col_bf'][:, 0:1], i,
                             start=st, stop=sp), wait=iw)
            self.op('pe', lambda h, i=sq, st=(dc == 0), sp=(dc == 15):
                    h.matmul(bsq[0:1, :], b['ones_col_bf'][:, 0:1], i,
                             start=st, stop=sp),
                    wait=[(A, a0 + dc + 1)], inc=P)
            pe_marks.append(P.v)
        r = b['rows']
        mu, ex2 = r[:, 0:TB], r[:, TB:2 * TB]
        var, rstd, brow = r[:, 2 * TB:3 * TB], r[:, 3 * TB:4 * TB], r[:, 4 * TB:5 * TB]
        self.op('dve', lambda h: h.tensor_scalar_mul(mu, bsum[0:1, :], 1.0 / 2048),
                wait=[(P, P.v)], inc=V)
        self.op('dve', lambda h: h.tensor_scalar_mul(ex2, bsq[0:1, :], 1.0 / 2048), inc=V)
        self.drain('dve')
        self.op('dve', lambda h: h.tensor_mul(var, mu, mu), inc=V)
        self.drain('dve')
        self.op('dve', lambda h: h.tensor_sub(var, ex2, var), inc=V)
        self.op('act', lambda h: h.activation(var, var, AF.Ln, bias=float(EPS)),
                wait=[(V, V.v)], inc=A)
        self.drain('act')
        self.op('act', lambda h: h.activation(rstd, var, AF.Exp, scale=-0.5), inc=A)
        self.drain('dve')
        self.op('dve', lambda h: h.scalar_tensor_tensor(brow, mu, -1.0, rstd,
                                                        ALU.mult, ALU.mult),
                wait=[(A, A.v)], inc=V)
        self.op('pe', lambda h: h.matmul(ba[:, :], b['ones_row_f32'][0:1, :],
                                         rstd, start=True, stop=True),
                wait=[(V, V.v)])
        self.op('pe', lambda h: h.matmul(bb[:, :], b['ones_row_f32'][0:1, :],
                                         brow, start=True, stop=True), inc=P)
        pv = P.v
        for dc0 in range(0, 16, 2):
            for j in range(2):
                dc = dc0 + j
                xsl = xh[:, dc * pitch_x + t0: dc * pitch_x + t0 + TB]
                self.op('dve', lambda h, x=xsl, j=j: h.tensor_mul(
                    b['ln_tmp'][:, j * TB:(j + 1) * TB], x, ba[:, :]),
                    wait=[(P, pv)] if dc == 0 else (), inc=V)
            self.drain('dve')
            for j in range(2):
                dc = dc0 + j
                ysl = yT[:, dc * pitch_y + (ytb0 + tb) * TB:
                         dc * pitch_y + (ytb0 + tb) * TB + TB]
                self.op('dve', lambda h, y=ysl, j=j: h.tensor_add(
                    y, b['ln_tmp'][:, j * TB:(j + 1) * TB], bb[:, :]), inc=V)
            self.drain('dve')
        self.waitop('pe', [(V, V.v)])


def emit_gram(k, xhbm, epart_out):
    b = k.bufs
    P, A, V, S = k.sem('P'), k.sem('A'), k.sem('V'), k.sem('S')
    SO = k.sem('SO')
    for th in range(2):
        s0 = S.v
        for dc in range(16):
            k.op('sy', lambda h, d=dc, t=th: h.dma_start(
                out=b['xh'][:, d * TH:(d + 1) * TH],
                in_=xhbm[d * 128:(d + 1) * 128, t * TH:(t + 1) * TH]),
                wait=[(V, k.gram_xh_guard)] if (dc == 0 and th > 0) else (),
                inc=S, inc_n=16)
        k.waitop('dve', [(S, S.v)])
        emit_ln_gen(k, b['xh'], b['ytmp'], 2, TH, TH, 0)
        k.gram_xh_guard = k.sem('V').v
        for tloc in range(8):
            tt = th * 8 + tloc
            for dc in range(16):
                k.op('dve', lambda h, tl=tloc, d=dc: h.tensor_copy(
                    b['yfc'][:, :],
                    b['ytmp'][:, d * TH + tl * 128:d * TH + (tl + 1) * 128]),
                    wait=[(P, k.yfc_guard)] if k.yfc_guard > 0 else (), inc=V)
                vv = V.v
                k.op('pe', lambda h: h.transpose(
                    k.psum['b4'][:, 0:128], b['yfc'][:, :], b['id_f32'][:, :]),
                    wait=[(V, vv)], inc=P)
                pv = P.v
                k.yfc_guard = pv
                k.op('dve', lambda h, t_=tt, d=dc: h.tensor_copy(
                    b['y_tm'][:, (t_ * 16 + d) * 128:(t_ * 16 + d + 1) * 128],
                    k.psum['b4'][:, 0:128]),
                    wait=[(P, pv)], inc=V)
                k.drain('dve')
    vfin = k.sem('V').v
    vmarks = [0, 0]
    somarks = [0, 0]
    first = True
    for ic in range(16):
        for jb in range(4):
            par = (ic * 4 + jb) % 2
            bank = k.psum['b5'] if par == 0 else k.psum['b6']
            for ct in range(16):
                w = []
                if first and ct == 0:
                    w = [(V, vfin)]
                    first = False
                elif ct == 0 and vmarks[par] > 0:
                    w = [(V, vmarks[par])]
                k.op('pe', lambda h, ic=ic, jb=jb, ct=ct, bank=bank: h.matmul(
                    bank[:, :],
                    b['y_tm'][:, (ct * 16 + ic) * 128:(ct * 16 + ic + 1) * 128],
                    b['y_tm'][:, ct * 2048 + jb * 512: ct * 2048 + jb * 512 + 512],
                    start=(ct == 0), stop=(ct == 15)), wait=w,
                    inc=(P if ct == 15 else None))
            pv = P.v
            ew = [(P, pv)]
            if somarks[par] > 0:
                ew.append((SO, somarks[par]))
            k.op('dve', lambda h, bank=bank, par=par: h.tensor_copy(
                b['egr'][:, par * TB:(par + 1) * TB], bank[:, :]),
                wait=ew, inc=V)
            vmarks[par] = k.sem('V').v
            vv = k.sem('V').v
            k.op('sy', lambda h, ic=ic, jb=jb, par=par: h.dma_start(
                out=epart_out[ic * 128:(ic + 1) * 128, jb * TB:(jb + 1) * TB],
                in_=b['egr'][:, par * TB:(par + 1) * TB]),
                wait=[(V, vv)], inc=SO, inc_n=16)
            somarks[par] = k.sem('SO').v


def emit_attnb_post(k, eparts, esum_hbm, x2in, outx):
    b = k.bufs
    P, A, V, S = k.sem('P'), k.sem('A'), k.sem('V'), k.sem('S')
    SO = k.sem('SO')
    ep4 = b['xh']
    Et = b['Et']; Et2 = b['Et2']; ebig = b['ebig']
    so_guard = 0
    for it in range(16):
        sv0 = S.v
        for p_ in range(4):
            w = []
            if p_ == 0 and it > 0:
                w = [(V, k.p1_guard)]
            if p_ == 0 and so_guard > 0:
                w = w + [(SO, so_guard)]
            k.op('sy', lambda h, it=it, p_=p_: h.dma_start(
                out=ep4[:, p_ * 2048:(p_ + 1) * 2048],
                in_=eparts[p_][it * 128:(it + 1) * 128, :]),
                wait=w, inc=S, inc_n=16)
        sv = S.v
        k.op('dve', lambda h: h.tensor_add(Et, ep4[:, 0:2048], ep4[:, 2048:4096]),
             wait=[(S, sv)], inc=V)
        k.op('dve', lambda h: h.tensor_add(Et2, ep4[:, 4096:6144], ep4[:, 6144:8192]),
             inc=V)
        k.drain('dve')
        k.op('dve', lambda h: h.tensor_add(Et, Et, Et2), inc=V)
        k.drain('dve')
        k.op('dve', lambda h, it=it: h.reduce_max(
            b['mpcols'][:, it:it + 1], Et, axis=AX, negate=True), inc=V)
        k.drain('dve')
        k.op('act', lambda h, it=it: h.activation(
            ebig, Et, AF.Exp, bias=b['mpcols'][:, it:it + 1]),
            wait=[(V, V.v)], inc=A)
        k.op('dve', lambda h: h.reduce_sum(b['scol'], ebig, axis=AX),
             wait=[(A, A.v)], inc=V)
        k.drain('dve')
        k.op('act', lambda h: h.activation(b['scol2'], b['scol'], AF.Ln),
             wait=[(V, V.v)], inc=A)
        k.op('dve', lambda h, it=it: h.tensor_sub(
            b['mpcols'][:, it:it + 1], b['scol2'], b['mpcols'][:, it:it + 1]),
            wait=[(A, A.v)], inc=V)
        k.p1_guard = k.sem('V').v
        k.op('sy', lambda h, it=it: h.dma_start(
            out=esum_hbm[it * 128:(it + 1) * 128, :], in_=Et),
            wait=[(V, k.p1_guard)], inc=SO, inc_n=16)
        so_guard = k.sem('SO').v
    for it in range(16):
        vv = V.v
        k.op('pe', lambda h, it=it: h.transpose(
            k.psum['b7'][0:1, 0:128], b['mpcols'][:, it:it + 1], b['id_f32'][:, :]),
            wait=[(V, vv)], inc=P)
        pv = P.v
        k.op('dve', lambda h, it=it: h.tensor_copy(
            b['mprow'][0:1, it * 128:(it + 1) * 128], k.psum['b7'][0:1, 0:128]),
            wait=[(P, pv)], inc=V)
    k.drain('dve')
    for th in range(2):
        for dc in range(16):
            w = [(V, k.sem('V').v)] if (dc == 0) else ()
            k.op('sy', lambda h, d=dc, t=th: h.dma_start(
                out=b['xh'][:, d * TH:(d + 1) * TH],
                in_=x2in[d * 128:(d + 1) * 128, t * TH:(t + 1) * TH]),
                wait=w, inc=S, inc_n=16)
        k.waitop('dve', [(S, S.v)])
        emit_ln_gen(k, b['xh'], b['y_full'], 2, TH, T, ytb0=2 * th)
    k.waitop('sy', [(SO, SO.v)])
    yv = k.sem('V').v
    banks = [k.psum['b0'], k.psum['b1'], k.psum['b2'], k.psum['b3']]
    ab_guard = [0, 0]
    so2 = 0
    SXR = k.sem('SXR')
    SE0, SE1 = k.sem('SE0'), k.sem('SE1')
    SEs = [SE0, SE1]
    for ic in range(16):
        xw = [(V, k.p3_guard), (SO, SO.v)] if k.p3_guard > 0 else [(V, yv)]
        k.op('sy', lambda h, ic=ic: h.dma_start(
            out=b['xrow'], in_=x2in[ic * 128:(ic + 1) * 128, :]),
            wait=xw, inc=SXR, inc_n=16)
        sx = SXR.v
        k.op('pe', lambda h, ic=ic: h.matmul(
            k.psum['b6'][:, 0:128], b['ones_row_f32'][0:1, :],
            b['mprow'][0:1, ic * 128:(ic + 1) * 128], start=True, stop=True),
            wait=[(V, yv), (V, k.p3_mbc)] if k.p3_mbc > 0 else [(V, yv)], inc=P)
        mb_pv = P.v
        for jt in range(16):
            par = jt % 2
            ew = [(V, k.sub_marks[par])] if k.sub_marks[par] > 0 else []
            k.op('sy', lambda h, ic=ic, jt=jt, par=par: h.dma_start(
                out=b['etile'][:, par * 128:(par + 1) * 128],
                in_=AP(tensor=esum_hbm.tensor,
                       offset=(jt * 128) * 2048 + ic * 128,
                       ap=[[2048, 128], [1, 128]])),
                wait=ew, inc=SEs[par], inc_n=16)
            se = SEs[par].v
            dw = [(SEs[par], se), (P, mb_pv)]
            if k.exp_marks[par] > 0:
                dw.append((A, k.exp_marks[par]))
            k.op('dve', lambda h, par=par: h.tensor_sub(
                b['asub'][:, par * 128:(par + 1) * 128],
                b['etile'][:, par * 128:(par + 1) * 128], k.psum['b6'][:, 0:128]),
                wait=dw, inc=V)
            k.sub_marks[par] = k.sem('V').v
            aw = [(V, k.sub_marks[par])]
            if k.mm_marks[par] > 0:
                aw.append((P, k.mm_marks[par]))
            k.op('act', lambda h, par=par: h.activation(
                b['abuf'][:, par * 128:(par + 1) * 128],
                b['asub'][:, par * 128:(par + 1) * 128], AF.Exp),
                wait=aw, inc=A)
            av = A.v
            k.exp_marks[par] = av
            for cb in range(4):
                k.op('pe', lambda h, ic=ic, jt=jt, cb=cb, par=par: h.matmul(
                    banks[cb][:, :], b['abuf'][:, par * 128:(par + 1) * 128],
                    b['y_full'][:, jt * T + cb * TB: jt * T + (cb + 1) * TB],
                    start=(jt == 0), stop=(jt == 15)),
                    wait=[(A, av)] if cb == 0 else (),
                    inc=(P if cb == 3 else None))
            k.mm_marks[par] = k.sem('P').v
        pv = P.v
        for cb in range(4):
            k.op('dve', lambda h, cb=cb: h.tensor_add(
                b['xrow'][:, cb * TB:(cb + 1) * TB],
                b['xrow'][:, cb * TB:(cb + 1) * TB], banks[cb][:, :]),
                wait=[(P, pv), (SXR, sx)] if cb == 0 else (), inc=V)
        k.p3_guard = k.sem('V').v
        k.p3_mbc = k.p3_guard
        k.op('sy', lambda h, ic=ic: h.dma_start(
            out=outx[ic * 128:(ic + 1) * 128, :], in_=b['xrow']),
            wait=[(V, k.p3_guard)], inc=SO, inc_n=16)

def _common_small(k, tens):
    b = k.bufs
    S, V = k.sem('S'), k.sem('V')
    k.emit_const_init(tens['sel8c'], tens['idc'])
    k.op('dve', lambda h: h.tensor_copy(b['id_bf'], b['id_f32']),
         wait=[(S, k.init_done[0])], inc=V)
    k.init_done = (k.init_done[0], V.v)


def alloc_small(k, rows_tb=5):
    nc = k.nc
    al = nc.alloc_sbuf_tensor
    b = k.bufs
    b['ones_col_bf'] = al('ones_col_bf', [128, 1], BF16)
    b['ones_row_f32'] = al('ones_row_f32', [1, 128], F32)
    b['id_f32'] = al('id_f32', [128, 128], F32)[:]
    b['id_bf'] = al('id_bf', [128, 128], BF16)[:]
    b['ln_xbf'] = al('ln_xbf', [128, 2 * TB], BF16)
    b['ln_sq'] = al('ln_sq', [128, 2 * TB], BF16)
    b['ln_tmp'] = al('ln_tmp', [128, 2 * TB], F32)
    b['rows'] = al('rows', [1, rows_tb * TB], F32)
    b['sel8'] = al('sel8', [8, 8 * 128], F32)
    ps = nc.alloc_psum_tensor
    for i in range(8):
        k.psum[f'b{i}'] = ps(f'psb{i}', [128, TB], F32)


def build_attn(nc, t):
    k = K(nc)
    alloc_small(k, rows_tb=8)
    al = nc.alloc_sbuf_tensor
    b = k.bufs
    b['acols'] = al('acols', [128, 8], F32)[:]
    b['demb'] = al('demb', [128, 4], F32)[:]
    b['vemb0r'] = al('vemb0r', [128, 4], F32)[:]
    b['vemb1r'] = al('vemb1r', [128, 4], F32)[:]
    b['tinyf'] = al('tinyf', [1, 8], F32)[:]
    b['vcop'] = al('vcop', [32, 128], F32)[:]
    b['cinfo'] = al('cinfo', [1, 256], I32)[:]
    b['q_raw'] = al('q_raw', [128, SS], BF16)[:]
    b['kv_raw'] = al('kv_raw', [64, SS], BF16)[:]
    b['vs_tm'] = al('vs_tm', [128, 128 * 36], BF16)[:]
    b['kwbuf'] = al('kwbuf', [128, 2 * 8192], BF16)[:]
    b['qwbuf'] = al('qwbuf', [128, 8 * TB], BF16)[:]
    b['ebuf'] = al('ebuf', [128, 4 * TB], BF16)[:]
    b['vsblk'] = al('vsblk', [128, 64 * 36], BF16)[:]
    b['outat'] = al('outat', [128, 16 * TB], BF16)[:]
    b['xblk'] = al('xblk', [128, 16 * TB], F32)[:]
    b['ytile'] = b['outat']
    b['crow'] = al('crow', [1, 4 * TB], BF16)[:]
    b['crow_f'] = b['rows'][:, 0:2048]
    b['qp1sb'] = b['rows'][:, 2048:4096]
    b['crow_bc'] = al('crow_bc', [128, 512], F32)[:]
    b['kvec2d'] = al('kvec2d', [128, 256], BF16)[:]
    b['kvec2f'] = al('kvec2f', [128, 256], F32)[:]
    b['ones_row_bf'] = al('ones_row_bf', [1, 128], BF16)[:]
    b['wq_s'] = al('wq_s', [128, 2048], BF16)[:]
    b['wk_s'] = al('wk_s', [128, 512], BF16)[:]
    b['wv_s'] = al('wv_s', [128, 512], BF16)[:]
    b['wo_s'] = al('wo_s', [128, 8192], BF16)[:]
    b['pv_sb'] = al('pv_sb', [128, TB], F32)[:]
    k.vs_stage = t['vs_stage']
    k.loc_q, k.loc_k = t['loc_q'], t['loc_k']
    k.loc_vs, k.loc_kv = t['loc_vs'], t['loc_kv']
    k.qdram, k.kvdram = t['qdram'], t['kvdram']
    k.qsel, k.kvsel = t['qsel'], t['kvsel']
    _common_small(k, t)
    S, W, V, P = k.sem('S'), k.sem('W'), k.sem('V'), k.sem('P')
    k.op('gp', lambda h: h.dma_start(out=b['wq_s'], in_=t['wq_h']), inc=W, inc_n=16)
    k.op('gp', lambda h: h.dma_start(out=b['wk_s'], in_=t['wk_h']), inc=W, inc_n=16)
    k.op('gp', lambda h: h.dma_start(out=b['wv_s'], in_=t['wv_h']), inc=W, inc_n=16)
    k.op('gp', lambda h: h.dma_start(out=b['wo_s'], in_=t['wo_h']), inc=W, inc_n=16)
    k.op('sy', lambda h: h.dma_start(out=b['cinfo'], in_=t['ci']), inc=S, inc_n=16)
    k.op('sy', lambda h: h.dma_start(out=b['vemb0r'], in_=t['v0r']), inc=S, inc_n=16)
    k.op('sy', lambda h: h.dma_start(out=b['vemb1r'], in_=t['v1r']), inc=S, inc_n=16)
    k.waitop('dve', [(S, S.v), (W, W.v)])
    k.waitop('pe', [(S, S.v), (W, W.v)])
    emit_attn_consts(k, t['tauh'])
    emit_attn_prologue(k, t['xtf'], t['qdram'], t['kvdram'], t['kvec'])
    emit_attn_main(k, b['cinfo'], t['kvec2'], t['qp1'])
    emit_attn_final(k, t['xin'], t['outx'])
    SO = k.sem('SO')
    k.waitop('sy', [(SO, SO.v)])
    k.waitop('gp', [(SO, SO.v)])
    with nc.Block() as blk:
        k.finalize(blk)
    return k


def build_ffn(nc, t):
    k = K(nc)
    alloc_small(k)
    al = nc.alloc_sbuf_tensor
    b = k.bufs
    b['lt'] = al('lt', [128, 8], F32)
    b['lte'] = al('lte', [128, 8], F32)
    b['ltp'] = al('ltp', [128, 8], F32)
    b['ltm'] = al('ltm', [128, 8], F32)
    b['ltw'] = al('ltw', [128, 8], F32)
    b['cols'] = al('cols', [128, 8], F32)
    b['wslab'] = [al('wslab0', [128, 2048], BF16), al('wslab1', [128, 2048], BF16)]
    b['dslab'] = [al('dslab0', [128, 1408], BF16), al('dslab1', [128, 1408], BF16)]
    b['tmp_silu'] = al('tmp_silu', [128, 4 * TB], BF16)
    b['h'] = al('h', [128, 11 * TH], BF16)
    b['sig_bc'] = al('sig_bc', [128, TH], BF16)
    b['we_bc'] = al('we_bc', [128, TH], BF16)
    b['w_rows'] = al('w_rows', [8, TH], F32)
    b['rl'] = al('rl', [9, TH], F32)
    b['xh'] = al('xh', [128, 16 * TH], F32)[:]
    b['yT'] = al('yT', [128, 16 * TH], BF16)[:]
    _common_small(k, t)
    S, SO2, V = k.sem('S'), k.sem('SOF'), k.sem('V')
    for th in range(2):
        for dc in range(16):
            w = [(SO2, SO2.v)] if dc == 0 else ()
            k.op('sy', lambda h, d=dc, t_=th: h.dma_start(
                out=b['xh'][:, d * TH:(d + 1) * TH],
                in_=t['xin'][d * 128:(d + 1) * 128, t_ * TH:(t_ + 1) * TH]),
                wait=w, inc=S, inc_n=16)
        k.waitop('dve', [(S, S.v)])
        emit_ln_gen(k, b['xh'], b['yT'], 2, TH, TH)
        k.emit_router(b['yT'], t['rts'], th)
        hg = 0
        for q in range(4):
            gu_slabs = [t['gus'][22 * q + i] for i in range(22)]
            down_slabs = [t['dws'][d][:, 11 * q * 128:(11 * q + 11) * 128]
                          for d in range(16)]
            hg = k.emit_quarter(b['xh'], b['yT'], gu_slabs, down_slabs, 11,
                                b['sig_bc'], hg)
        for e in range(8):
            k.emit_we_bcast(e)
            gu_slabs = [t['e13'][22 * e + i] for i in range(22)]
            down_slabs = [t['e2'][16 * e + d] for d in range(16)]
            hg = k.emit_quarter(b['xh'], b['yT'], gu_slabs, down_slabs, 11,
                                b['we_bc'], hg)
        k.waitop('sy', [(V, V.v)])
        for dc in range(16):
            k.op('sy', lambda h, d=dc, t_=th: h.dma_start(
                out=t['outx'][d * 128:(d + 1) * 128, t_ * TH:(t_ + 1) * TH],
                in_=b['xh'][:, d * TH:(d + 1) * TH]),
                inc=SO2, inc_n=16)
        k.waitop('sy', [(SO2, SO2.v)])
    k.waitop('gp', [(SO2, SO2.v)])
    with nc.Block() as blk:
        k.finalize(blk)
    return k


def build_gram(nc, t):
    k = K(nc)
    alloc_small(k)
    al = nc.alloc_sbuf_tensor
    b = k.bufs
    b['xh'] = al('xh', [128, 16 * TH], F32)[:]
    b['ytmp'] = al('ytmp', [128, 16 * TH], BF16)[:]
    b['y_tm'] = al('y_tm', [128, 16 * 2048], BF16)[:]
    b['yfc'] = al('yfc', [128, 128], F32)[:]
    b['egr'] = al('egr', [128, 1024], F32)[:]
    _common_small(k, t)
    k.gram_xh_guard = 0
    k.yfc_guard = 0
    emit_gram(k, t['xin'], t['epart'])
    SO = k.sem('SO')
    k.waitop('sy', [(SO, SO.v)])
    k.waitop('gp', [(SO, SO.v)])
    with nc.Block() as blk:
        k.finalize(blk)
    return k


def build_bpost(nc, t):
    k = K(nc)
    alloc_small(k)
    al = nc.alloc_sbuf_tensor
    b = k.bufs
    b['xh'] = al('xh', [128, 16 * TH], F32)[:]
    b['y_full'] = al('y_full', [128, 16 * 2048], BF16)[:]
    b['Et'] = al('Et', [128, 2048], F32)[:]
    b['Et2'] = al('Et2', [128, 2048], F32)[:]
    b['ebig'] = al('ebig', [128, 2048], BF16)[:]
    b['mpcols'] = al('mpcols', [128, 16], F32)[:]
    b['mprow'] = al('mprow', [1, 2048], F32)[:]
    b['scol'] = al('scol', [128, 1], F32)[:]
    b['scol2'] = al('scol2', [128, 1], F32)[:]
    b['etile'] = al('etile', [128, 256], F32)[:]
    b['asub'] = al('asub', [128, 256], F32)[:]
    b['abuf'] = al('abuf', [128, 256], BF16)[:]
    b['xrow'] = al('xrow', [128, 2048], F32)[:]
    _common_small(k, t)
    k.p1_guard = 0
    k.p3_guard = 0
    k.p3_mbc = 0
    k.sub_marks = [0, 0]
    k.exp_marks = [0, 0]
    k.mm_marks = [0, 0]
    emit_attnb_post(k, t['eps'], t['esum'], t['xin'], t['outx'])
    SO = k.sem('SO')
    k.waitop('sy', [(SO, SO.v)])
    k.waitop('gp', [(SO, SO.v)])
    with nc.Block() as blk:
        k.finalize(blk)
    return k



_CACHE = {}


def _host_prep(inputs):
    if 'prep' in _CACHE:
        return _CACHE['prep']
    gate_w = np.asarray(inputs['L0_ffn_gate_w'], np.float32)
    sgate_w = np.asarray(inputs['L0_ffn_shared_gate_w'], np.float32)
    sgu_w = np.asarray(inputs['L0_ffn_shared_gate_up_w'], np.float32)
    sdown_w = np.asarray(inputs['L0_ffn_shared_down_w'], np.float32)
    w13 = np.asarray(inputs['L0_ffn_w13'], np.float32)
    w2 = np.asarray(inputs['L0_ffn_w2'], np.float32)
    Wq = np.asarray(inputs['L0_attn_a_Wq'], np.float32)
    Wk = np.asarray(inputs['L0_attn_a_Wk'], np.float32)
    Wv = np.asarray(inputs['L0_attn_a_Wv'], np.float32)
    Wo = np.asarray(inputs['L0_attn_a_Wo'], np.float32)
    v0e = np.asarray(inputs['L0_attn_a_v_emb0'], np.float32)
    v1e = np.asarray(inputs['L0_attn_a_v_emb1'], np.float32)
    chunk_order = []
    for q in range(4):
        for ip in range(11):
            chunk_order += [11 * q + ip, 44 + 11 * q + ip]
    GUS = host_gu_slabs(sgu_w, chunk_order)
    RTS = np.concatenate([host_partial_slab(gate_w),
                          host_partial_slab(sgate_w)], 1)
    DWS = host_down_slabs(sdown_w, 44)
    e_order = []
    for ic in range(11):
        e_order += [ic, 11 + ic]
    E13 = np.concatenate([host_gu_slabs(w13[e], e_order) for e in range(8)], 0)
    E2 = np.concatenate([host_down_slabs(w2[e], 11) for e in range(8)], 0)
    SEL8 = np.zeros((8, 8 * 128), np.float32)
    for e in range(8):
        SEL8[e, e * 128:(e + 1) * 128] = 1.0
    IDN = np.eye(128, dtype=np.float32)
    WQS = host_partial_slab(Wq)
    WKS = host_partial_slab(Wk)
    WVS = host_partial_slab(Wv)
    WOS = np.zeros((128, 4 * 2048), np.float32)
    V0R = np.zeros((128, 4), np.float32)
    V1R = np.zeros((128, 4), np.float32)
    for g in range(4):
        for j in range(4):
            for v in range(8):
                WOS[32 * j + 1 + v, g * 2048:(g + 1) * 2048] = Wo[:, (4 * g + j) * 8 + v]
                V0R[32 * j + 1 + v, g] = v0e[(4 * g + j) * 8 + v]
                V1R[32 * j + 1 + v, g] = v1e[(4 * g + j) * 8 + v]
    import ml_dtypes
    KVEC = np.concatenate([np.full(8192, -3e38, np.float32),
                           np.arange(8192, dtype=np.float32)]).reshape(1, 16384)
    KVEC2 = np.ascontiguousarray(KVEC.reshape(128, 128).T).astype(ml_dtypes.bfloat16)
    KVEC = KVEC.astype(ml_dtypes.bfloat16)
    TAU = np.asarray(inputs['L0_attn_a_tau'], np.float32).reshape(1, 1)
    prep = dict(GUS=GUS, RTS=RTS, DWS=DWS, E13=E13, E2=E2, SEL8=SEL8, IDN=IDN,
                WQS=WQS, WKS=WKS, WVS=WVS, WOS=WOS, V0R=V0R, V1R=V1R,
                KVEC=KVEC, KVEC2=KVEC2, TAU=TAU)
    _CACHE['prep'] = prep
    return prep


def _core_info(c):
    c2, bb = c % 4, c // 4
    Js = sorted([c2, 7 - c2, 8 + c2, 15 - c2])
    kA = [4 * J + 4 for J in Js]
    ci = np.zeros(256, np.int32)
    for blk in range(4):
        ci[blk] = Js[blk] * 512
        ci[4 + blk] = KZ + (kA[blk] - PADK[blk]) * 128
        ci[8 + blk] = (64 + kA[blk] - PADK[blk]) * 36
        ci[12 + blk] = 64 + kA[blk] - PADK[blk]
    qp1 = np.zeros((1, 2048), np.float32)
    for blk in range(4):
        qp1[0, blk * 512:(blk + 1) * 512] = Js[blk] * 512 + np.arange(512) + 1
    rows = np.concatenate([np.arange(J * 512, (J + 1) * 512) for J in Js])
    return bb, rows, ci.reshape(1, 256), qp1


def _mk_attn_prog(prep):
    nc = bass.Bass("TRN2", target_bir_lowering=False, debug=False)
    t = {}
    t['xtf'] = nc.dram_tensor('xtf', [2048, 8192], F32, kind="ExternalInput").ap()
    t['xin'] = nc.dram_tensor('xin', [2048, 2048], F32, kind="ExternalInput").ap()
    t['outx'] = nc.dram_tensor('outx', [2048, 2048], F32, kind="ExternalOutput").ap()
    t['ci'] = nc.dram_tensor('ci', [1, 256], I32, kind="ExternalInput").ap()
    t['qp1'] = nc.dram_tensor('qp1', [1, 2048], F32, kind="ExternalInput").ap()
    t['kvec'] = nc.dram_tensor('kvec', [1, 16384], BF16, kind="ExternalInput").ap()
    for n, sh in [('wq_h', (128, 2048)), ('wk_h', (128, 512)), ('wv_h', (128, 512)),
                  ('wo_h', (128, 8192)), ('v0r', (128, 4)), ('v1r', (128, 4)),
                  ('tauh', (1, 1)), ('sel8c', (8, 1024)), ('idc', (128, 128))]:
        t[n] = nc.dram_tensor(n, list(sh), F32, kind="ExternalInput").ap()
    t['kvec2'] = nc.dram_tensor('kvec2', [128, 128], BF16, kind="ExternalInput").ap()
    t['qdram'] = nc.dram_tensor('qdram', [128, SS], BF16, kind="Internal").ap()
    t['kvdram'] = nc.dram_tensor('kvdram', [64, KVW], BF16, kind="Internal").ap()
    t['qsel'] = nc.dram_tensor('qsel', [128, QSP], BF16, kind="Internal").ap()
    t['kvsel'] = nc.dram_tensor('kvsel', [64, KSP], BF16, kind="Internal").ap()
    t['vs_stage'] = nc.dram_tensor('vs_stage', [128, 128 * 36], BF16, kind="Internal").ap()
    t['loc_q'] = nc.dram_tensor('loc_q', [4096, 4 * 512], BF16, kind="Internal").ap()
    t['loc_k'] = nc.dram_tensor('loc_k', [1024, 160 * 128], BF16, kind="Internal").ap()
    t['loc_vs'] = nc.dram_tensor('loc_vs', [128, 160 * 36], BF16, kind="Internal").ap()
    t['loc_kv'] = nc.dram_tensor('loc_kv', [1, 160 * 128], BF16, kind="Internal").ap()
    build_attn(nc, t)
    return nc


def _mk_ffn_prog(prep):
    nc = bass.Bass("TRN2", target_bir_lowering=False, debug=False)
    t = {}
    t['xin'] = nc.dram_tensor('xin', [2048, 2048], F32, kind="ExternalInput").ap()
    t['outx'] = nc.dram_tensor('outx', [2048, 2048], F32, kind="ExternalOutput").ap()
    for n, a in [('gus', prep['GUS']), ('rts', prep['RTS']), ('dws', prep['DWS']),
                 ('e13', prep['E13']), ('e2', prep['E2']),
                 ('sel8c', prep['SEL8']), ('idc', prep['IDN'])]:
        t[n] = nc.dram_tensor(n, list(a.shape), F32, kind="ExternalInput").ap()
    build_ffn(nc, t)
    return nc


def _mk_gram_prog(prep):
    nc = bass.Bass("TRN2", target_bir_lowering=False, debug=False)
    t = {}
    t['xin'] = nc.dram_tensor('xin', [2048, 2048], F32, kind="ExternalInput").ap()
    t['epart'] = nc.dram_tensor('epart', [2048, 2048], F32, kind="ExternalOutput").ap()
    t['sel8c'] = nc.dram_tensor('sel8c', [8, 1024], F32, kind="ExternalInput").ap()
    t['idc'] = nc.dram_tensor('idc', [128, 128], F32, kind="ExternalInput").ap()
    build_gram(nc, t)
    return nc


def _mk_bpost_prog(prep):
    nc = bass.Bass("TRN2", target_bir_lowering=False, debug=False)
    t = {}
    t['xin'] = nc.dram_tensor('xin', [2048, 2048], F32, kind="ExternalInput").ap()
    for i in range(4):
        t[f'ep{i}'] = nc.dram_tensor(f'ep{i}', [2048, 2048], F32, kind="ExternalInput").ap()
    t['eps'] = [t[f'ep{i}'] for i in range(4)]
    t['esum'] = nc.dram_tensor('esum', [2048, 2048], F32, kind="Internal").ap()
    t['outx'] = nc.dram_tensor('outx', [2048, 2048], F32, kind="ExternalOutput").ap()
    t['sel8c'] = nc.dram_tensor('sel8c', [8, 1024], F32, kind="ExternalInput").ap()
    t['idc'] = nc.dram_tensor('idc', [128, 128], F32, kind="ExternalInput").ap()
    build_bpost(nc, t)
    return nc


def kernel(**inputs):
    prep = _host_prep(inputs)
    x0 = np.asarray(inputs['L0_attn_a_hidden_states'], np.float32)
    XT = [np.ascontiguousarray(x0[b].T) for b in range(2)]
    infos = [_core_info(c) for c in range(8)]

    wfeed = {'gus': prep['GUS'], 'rts': prep['RTS'], 'dws': prep['DWS'],
             'e13': prep['E13'], 'e2': prep['E2'],
             'sel8c': prep['SEL8'], 'idc': prep['IDN']}

    nc_a = _mk_attn_prog(prep)
    feeds = []
    for c in range(8):
        bb, rows, ci, qp1 = infos[c]
        feeds.append({'xtf': XT[bb], 'xin': np.ascontiguousarray(x0[bb][rows].T),
                      'ci': ci, 'qp1': qp1, 'kvec': prep['KVEC'],
                      'kvec2': prep['KVEC2'],
                      'wq_h': prep['WQS'], 'wk_h': prep['WKS'],
                      'wv_h': prep['WVS'], 'wo_h': prep['WOS'],
                      'v0r': prep['V0R'], 'v1r': prep['V1R'],
                      'tauh': prep['TAU'], 'sel8c': prep['SEL8'],
                      'idc': prep['IDN']})
    r = run_bass_kernel_spmd(nc_a, feeds, core_ids=list(range(8)))
    x1sh = [np.ascontiguousarray(r.results[c]['outx']) for c in range(8)]

    nc_f = _mk_ffn_prog(prep)
    feeds = [dict(wfeed, xin=x1sh[c]) for c in range(8)]
    r = run_bass_kernel_spmd(nc_f, feeds, core_ids=list(range(8)))
    x2sh = [np.ascontiguousarray(r.results[c]['outx']) for c in range(8)]

    nc_g = _mk_gram_prog(prep)
    feeds = [{'xin': x2sh[c], 'sel8c': prep['SEL8'], 'idc': prep['IDN']}
             for c in range(8)]
    r = run_bass_kernel_spmd(nc_g, feeds, core_ids=list(range(8)))
    eparts = [np.ascontiguousarray(r.results[c]['epart']) for c in range(8)]

    nc_b = _mk_bpost_prog(prep)
    feeds = []
    for c in range(8):
        bb = c // 4
        fd = {'xin': x2sh[c], 'sel8c': prep['SEL8'], 'idc': prep['IDN']}
        for i, g in enumerate([4 * bb + j for j in range(4)]):
            fd[f'ep{i}'] = eparts[g]
        feeds.append(fd)
    r = run_bass_kernel_spmd(nc_b, feeds, core_ids=list(range(8)))
    x3sh = [np.ascontiguousarray(r.results[c]['outx']) for c in range(8)]

    feeds = [dict(wfeed, xin=x3sh[c]) for c in range(8)]
    r = run_bass_kernel_spmd(nc_f, feeds, core_ids=list(range(8)))

    out = np.zeros((2, 8192, 2048), np.float32)
    for c in range(8):
        bb, rows = infos[c][0], infos[c][1]
        out[bb][rows] = np.ascontiguousarray(r.results[c]['outx']).T
    return out
